# version 10
# baseline (speedup 1.0000x reference)
import contextlib
import math

import concourse.bass as bass
import concourse.mybir as mybir

F32 = mybir.dt.float32
BF16 = mybir.dt.bfloat16
AF = mybir.ActivationFunctionType
ALU = mybir.AluOpType
AX = mybir.AxisListType

MAGIC = float(3 * 2.0 ** 22)
S, D, FF, II = 2048, 1024, 3072, 4096
TC, P = 16, 128
KB = 1024


class Sem:
    def __init__(self, es, nc, name):
        self.h = es.enter_context(nc.semaphore(name))
        self.n = 0

    def inc(self, instr, k=1):
        instr.then_inc(self.h, k)
        self.n += k
        return instr


class KBuild:
    def __init__(self, debug=(), stop_after=None):
        self.debug = set(debug)
        self.stop_after = stop_after

    def build(self):
        es = contextlib.ExitStack()
        self.es = es
        nc = bass.Bass("TRN2", target_bir_lowering=False,
                       detect_race_conditions=False)
        self.nc = nc
        with es:
            self.declare()
            self.blk = es.enter_context(nc.Block())
            nc.gpsimd.dma_reset()
            for s in self.all_sems:
                nc.gpsimd.sem_clear(s.h)
            nc.vector.memset(self.ceps[:, 0:1], 1e-5)
            nc.vector.memset(self.ceps[:, 1:2], 1e-6)
            nc.all_engine_barrier()
            phases = [("A", self.phase_A), ("C1", self.phase_C1),
                      ("C2pre", self.phase_C2pre), ("C2", self.phase_C2),
                      ("D0", self.phase_D0), ("D1", self.phase_D1),
                      ("D2", self.phase_D2)]
            for name, fn in phases:
                fn()
                nc.all_engine_barrier()
                if self.stop_after == name:
                    break
            nc.sync.wait_ge(self.sq.h, self.sq.n)
            nc.gpsimd.wait_ge(self.gq.h, self.gq.n)
        return nc

    def declare(self):
        nc, es = self.nc, self.es
        dt = {}
        self.dram = dt
        ein = lambda n, s: nc.dram_tensor(n, s, F32, kind="ExternalInput").ap()
        dt["x"] = ein("x", [S, D])
        dt["wrmix"] = ein("wrmix", [D])
        dt["wrffn"] = ein("wrffn", [D])
        for n in ("Wg", "Wv", "Wd"):
            dt[n] = ein(n, [D, D])
        for n in ("bg", "bv", "bd"):
            dt[n] = ein(n, [D])
        dt["Wgate"] = ein("Wgate", [D, FF])
        dt["Wup"] = ein("Wup", [D, FF])
        dt["Wout"] = ein("Wout", [FF, D])
        dt["fwg"] = ein("fwg", [II, D])
        dt["fwu"] = ein("fwu", [II, D])
        dt["fwd"] = ein("fwd", [D, II])
        dt["y"] = nc.dram_tensor("y", [S, D], F32, kind="ExternalOutput").ap()
        dt["s_sp"] = nc.dram_tensor("s_sp", [S, D], F32, kind="Internal").ap()
        dt["qx_sp"] = nc.dram_tensor("qx_sp", [D, S], BF16, kind="Internal").ap()
        dt["qh_sp"] = nc.dram_tensor("qh_sp", [TC, 32, P, P], BF16,
                                     kind="Internal").ap()
        dt["qwd_sp"] = nc.dram_tensor("qwd_sp", [4, 8, P, 8, P], BF16,
                                      kind="Internal").ap()
        self.taps = {}
        tapspec = {
            "xn": ([P, TC * D], F32), "xatt": ([P, TC * D], F32),
            "s": ([S, D], F32), "xnb": ([P, 8, S], BF16), "qx": ([D, S], BF16),
            "stats": ([P, 352], F32), "qwgT": ([D, II], BF16),
            "qwdT": ([II, D], BF16), "h4": ([P, II], F32), "g4": ([P, II], F32), "qh": ([TC, 32, P, P], BF16),
        }
        for name in self.debug:
            shp, dty = tapspec[name]
            self.taps[name] = nc.dram_tensor("tap_" + name, shp, dty,
                                             kind="ExternalOutput").ap()

        self.sq = Sem(es, nc, "sq")
        self.gq = Sem(es, nc, "gq")
        self.sT = Sem(es, nc, "sT")
        self.sV = Sem(es, nc, "sV")
        self.sS = Sem(es, nc, "sS")
        self.sG = Sem(es, nc, "sG")
        self.all_sems = [self.sq, self.gq, self.sT, self.sV, self.sS, self.sG]

        sb = lambda n, shp, dty=F32: es.enter_context(nc.sbuf_tensor(n, shp, dty))
        self.ident_b = sb("ident_b", [P, P], BF16)
        self.ones128 = sb("ones128", [P, P], F32)
        self.identf = sb("identf", [P, P], F32)
        self.wrmix_c = sb("wrmix_c", [P, 8])
        self.wrffn_c = sb("wrffn_c", [P, 8])
        self.bg_c = sb("bg_c", [P, 8])
        self.bv_c = sb("bv_c", [P, 8])
        self.bd_c = sb("bd_c", [P, 8])
        self.st = {}
        for n in ("s1", "s2", "mm", "var", "r_ln", "nb_ln", "s2n", "r_mix",
                  "amax", "sxinv", "sx", "s2f", "r_ffn", "kq", "amaxh",
                  "m2", "gsc", "t0", "t1", "t2", "clipd"):
            self.st[n] = sb("st_" + n, [P, 16])
        self.wq = {}
        for n in ("sabs", "sabsB", "mwg", "mwu", "mwd", "thrg", "nthrg", "thru",
                  "nthru", "thrd", "nthrd"):
            self.wq[n] = sb("wq_" + n, [P, 1])
        self.junkA = sb("junkA", [P, 64])
        self.p1acc = {n: sb("p1acc_" + n, [P, 32]) for n in ("fwg", "fwu", "fwd")}
        self.p1_state = {}
        self.ceps = sb("ceps", [P, 2])
        nc.const_aps.register(F32, 1e-5, self.ceps[:, 0:1])
        nc.const_aps.register(F32, 1e-6, self.ceps[:, 1:2])
        self.ARN = es.enter_context(
            nc.sbuf_tensor("ARN", [P, 192 * KB], mybir.dt.uint8))
        self.arena_base = nc.lookup_mloc(self.ARN).addr
        self._vid = 0

    def sg(self, sem=None, k=1):
        i = self.nc.scalar.activation(self.junkA[:, 0:64], self.junkA[:, 0:64],
                                      AF.Identity)
        if sem is not None:
            sem.inc(i, k)
        return i

    def vg(self, sem=None, k=1):
        i = self.nc.vector.tensor_scalar(self.junkA.ap(), self.junkA.ap(),
                                         1.0, None, ALU.mult)
        if sem is not None:
            sem.inc(i, k)
        return i


    def p1_gen(self, wn, slabs, junk, modes, width):
        nc = self.nc
        V, SC, SY = nc.vector, nc.scalar, nc.sync
        sq, sV, sS = self.sq, self.sV, self.sS
        acc = self.p1acc[wn]
        nsl = (II * D) // (128 * width)
        free = [None, None]
        for s2_ in range(nsl):
            sb_ = s2_ % 2
            mode = modes[s2_ % len(modes)]
            if free[sb_] is not None:
                SY.wait_ge(free[sb_][0].h, free[sb_][1])
            sq.inc(SY.dma_start(
                out=slabs[sb_].ap(),
                in_=self.dram[wn].rearrange("a d -> (a d)").rearrange(
                    "(s p w) -> s p w", p=P, w=width)[s2_]), 16)
            if mode == "V":
                V.wait_ge(sq.h, sq.n)
                V.reduce_sum(acc[:, s2_:s2_ + 1], slabs[sb_].ap(),
                             AX.X, apply_absolute_value=True)
                self.vg()
                self.vg(sV)
                free[sb_] = (sV, sV.n)
            else:
                SC.wait_ge(sq.h, sq.n)
                SC.activation(junk[:, 0:width], slabs[sb_].ap(), AF.Abs,
                              accum_out=acc[:, s2_:s2_ + 1])
                self.sg()
                self.sg(sS)
                free[sb_] = (sS, sS.n)
            yield
        self.p1_state[wn] = (free, nsl)

    def pump(self, gen):
        if gen is not None:
            next(gen, None)

    def drain(self, gen):
        if gen is not None:
            for _ in gen:
                pass

    def view(self, shape, dtype, off_kb):
        self._vid += 1
        return self.nc.alloc_sbuf_tensor_at(
            f"v{self._vid}", shape, dtype,
            offset=self.arena_base + int(round(off_kb * KB)))

    def phase_A(self):
        nc, st, dram = self.nc, self.st, self.dram
        sq, gq, sT, sV, sS, sG = self.sq, self.gq, self.sT, self.sV, self.sS, self.sG
        V, SC, T, SY, G = nc.vector, nc.scalar, nc.tensor, nc.sync, nc.gpsimd

        self.xn_tm = self.view([P, TC, D], F32, 0)
        self.x_tm = self.view([P, TC, D], F32, 64)
        self.xnb = self.view([P, 8, S], BF16, 128)
        xn2b_q = self.view([P, 4, D], BF16, 160)
        qx_q = self.view([P, 4, D], BF16, 168)
        sqscr = self.view([P, D], F32, 176)
        qtmp = self.view([P, D], F32, 180)
        qxstage = self.view([P, 8, 512], BF16, 184)

        sq.inc(SY.dma_start(out=self.x_tm.ap(),
                            in_=dram["x"].rearrange("(a p) d -> p a d", p=P)), 16)
        for nm, dst in (("wrmix", self.wrmix_c), ("wrffn", self.wrffn_c),
                        ("bg", self.bg_c), ("bv", self.bv_c), ("bd", self.bd_c)):
            sq.inc(SY.dma_start(out=dst.ap(),
                                in_=dram[nm].rearrange("(c p) -> p c", p=P),
                                allow_slow_non_contiguous=True), 16)
        x_loaded = sq.n

        G.memset(self.ident_b.ap(), 0.0)
        G.affine_select(out=self.ident_b.ap(), in_=self.ident_b.ap(),
                        compare_op=ALU.not_equal, fill=1.0, base=0,
                        pattern=[[-1, P]], channel_multiplier=1)
        G.memset(self.identf.ap(), 0.0)
        G.affine_select(out=self.identf.ap(), in_=self.identf.ap(),
                        compare_op=ALU.not_equal, fill=1.0, base=0,
                        pattern=[[-1, P]], channel_multiplier=1)
        sG.inc(G.memset(self.ones128.ap(), 1.0))
        self.idents_ready = sG.n

        SC.memzero(st["s2"].ap())
        SC.memzero(st["s2n"].ap())
        SC.memzero(st["s2f"].ap())
        SC.wait_ge(sq.h, x_loaded)
        for tc in range(TC):
            i = SC.activation(sqscr.ap(), self.x_tm[:, tc, :], AF.Square,
                              accum_out=st["s2"][:, tc:tc + 1])
        sS.inc(i)
        s2_done = sS.n

        V.wait_ge(sq.h, x_loaded)
        for tc in range(TC):
            V.reduce_sum(st["s1"][:, tc:tc + 1], self.x_tm[:, tc, :], AX.X)
        V.wait_ge(sS.h, s2_done)
        self.vg()
        self.vg()
        V.tensor_scalar(st["mm"].ap(), st["s1"].ap(), 1.0 / D, None, ALU.mult)
        self.vg()
        V.tensor_tensor(st["t0"].ap(), st["mm"].ap(), st["mm"].ap(), ALU.mult)
        self.vg()
        V.tensor_scalar(st["t0"].ap(), st["t0"].ap(), -1.0, None, ALU.mult)
        self.vg()
        V.scalar_tensor_tensor(st["var"].ap(), st["s2"].ap(), 1.0 / D,
                               st["t0"].ap(), ALU.mult, ALU.add)
        self.vg(sV)
        var_done = sV.n

        SC.wait_ge(sV.h, var_done)
        SC.activation(st["t1"].ap(), st["var"].ap(), AF.Ln, bias=1e-5, scale=1.0)
        self.sg()
        SC.activation(st["r_ln"].ap(), st["t1"].ap(), AF.Exp, scale=-0.5)
        self.sg(sS)
        rln_done = sS.n

        V.wait_ge(sS.h, rln_done)
        V.tensor_tensor(st["nb_ln"].ap(), st["mm"].ap(), st["r_ln"].ap(), ALU.mult)
        self.vg()
        V.tensor_scalar(st["nb_ln"].ap(), st["nb_ln"].ap(), -1.0, None, ALU.mult)
        self.vg(sV)
        nb_done = sV.n

        SC.wait_ge(sV.h, nb_done)
        for tc in range(TC):
            i = SC.activation(self.xn_tm[:, tc, :], self.x_tm[:, tc, :],
                              AF.Identity, bias=st["nb_ln"][:, tc:tc + 1],
                              scale=st["r_ln"][:, tc:tc + 1])
        sS.inc(i)
        xn_done = sS.n
        for tc in range(TC):
            SC.activation(sqscr.ap(), self.xn_tm[:, tc, :], AF.Square,
                          accum_out=st["s2n"][:, tc:tc + 1])
        SC.activation(st["t1"].ap(), st["s2n"].ap(), AF.Ln, bias=1e-6, scale=1.0 / D)
        self.sg()
        SC.activation(st["r_mix"].ap(), st["t1"].ap(), AF.Exp, scale=-0.5)
        self.sg(sS)
        rmix_done = sS.n

        V.wait_ge(sS.h, xn_done)
        for tc in range(TC):
            V.reduce_max(st["amax"][:, tc:tc + 1], self.xn_tm[:, tc, :], AX.X,
                         apply_absolute_value=True)
        self.vg()
        V.tensor_scalar(st["sxinv"].ap(), st["amax"].ap(), 1e-5,
                        1.0 / 127.0, ALU.max, ALU.mult)
        self.vg(sV)
        sxinv_done = sV.n
        SC.wait_ge(sV.h, sxinv_done)
        SC.activation(st["t1"].ap(), st["sxinv"].ap(), AF.Ln)
        self.sg()
        SC.activation(st["sx"].ap(), st["t1"].ap(), AF.Exp, scale=-1.0)
        self.sg(sS)
        sx_done = sS.n

        with contextlib.ExitStack() as pes:
            pb = [pes.enter_context(nc.psum_tensor(f"pA{i}", [P, 512], BF16))
                  for i in range(4)]
            bank_free = [None, None, None, None]
            q_trans_done = None
            for q in range(4):
                V.wait_ge(sS.h, sx_done)
                if q_trans_done is not None:
                    V.wait_ge(sT.h, q_trans_done)
                for k in range(4):
                    tc = q * 4 + k
                    V.tensor_scalar(qtmp.ap(), self.xn_tm[:, tc, :],
                                    st["sx"][:, tc:tc + 1], MAGIC, ALU.mult, ALU.add)
                    i = V.tensor_scalar(qx_q[:, k, :], qtmp.ap(), -MAGIC,
                                        None, ALU.add)
                sV.inc(i)
                qxq_done = sV.n
                SC.wait_ge(sS.h, rmix_done)
                if q_trans_done is not None:
                    SC.wait_ge(sT.h, q_trans_done)
                for k in range(4):
                    tc = q * 4 + k
                    i = SC.activation(xn2b_q[:, k, :], self.xn_tm[:, tc, :],
                                      AF.Identity, scale=st["r_mix"][:, tc:tc + 1])
                sS.inc(i)
                x2b_done = sS.n

                T.wait_ge(sV.h, qxq_done)
                T.wait_ge(sS.h, x2b_done)
                T.wait_ge(sG.h, self.idents_ready)
                for fc in range(8):
                    b = fc % 2
                    if bank_free[b] is not None:
                        T.wait_ge(bank_free[b][0].h, bank_free[b][1])
                    for k in range(4):
                        i = T.transpose(pb[b][:, k * 128:(k + 1) * 128],
                                        xn2b_q[:, k, fc * 128:(fc + 1) * 128],
                                        self.ident_b.ap())
                    sT.inc(i)
                    tcnt = sT.n
                    SC.wait_ge(sT.h, tcnt)
                    sS.inc(SC.activation(self.xnb[:, fc, q * 512:(q + 1) * 512],
                                         pb[b].ap(), AF.Identity,
                                         scale=self.wrmix_c[:, fc:fc + 1]))
                    bank_free[b] = (sS, sS.n)
                for jc in range(8):
                    b = 2 + jc % 2
                    if bank_free[b] is not None:
                        T.wait_ge(bank_free[b][0].h, bank_free[b][1])
                    for k in range(4):
                        i = T.transpose(pb[b][:, k * 128:(k + 1) * 128],
                                        qx_q[:, k, jc * 128:(jc + 1) * 128],
                                        self.ident_b.ap())
                    sT.inc(i)
                    tcnt = sT.n
                    V.wait_ge(sT.h, tcnt)
                    sV.inc(V.tensor_copy(qxstage[:, jc, :], pb[b].ap()))
                    bank_free[b] = (sV, sV.n)
                stage_done = sV.n
                q_trans_done = sT.n
                SY.wait_ge(sV.h, stage_done)
                sq.inc(SY.dma_start(
                    out=dram["qx_sp"].rearrange("(a p) t -> p a t", p=P)[:, :, q * 512:(q + 1) * 512],
                    in_=qxstage.ap()), 16)
        self.qx_spilled = sq.n

        if self.taps:
            SY.wait_ge(sS.h, sS.n)
            SY.wait_ge(sV.h, sV.n)
            if "xn" in self.taps:
                sq.inc(SY.dma_start(out=self.taps["xn"],
                                    in_=self.xn_tm.ap().rearrange("p a d -> p (a d)")), 16)
            if "stats" in self.taps:
                for i2, nm in enumerate(("mm", "var", "r_ln", "r_mix", "amax",
                                         "sxinv", "sx", "s2", "s2n", "s1")):
                    sq.inc(SY.dma_start(out=self.taps["stats"][:, i2 * 16:(i2 + 1) * 16],
                                        in_=st[nm].ap()), 16)
            if "xnb" in self.taps:
                sq.inc(SY.dma_start(out=self.taps["xnb"], in_=self.xnb.ap()), 16)
            if "qx" in self.taps:
                SY.wait_ge(sq.h, self.qx_spilled)
                sq.inc(SY.dma_start(out=self.taps["qx"], in_=dram["qx_sp"]), 16)

    def phase_C1(self):
        nc, st, dram = self.nc, self.st, self.dram
        sq, gq, sT, sV, sS, sG = self.sq, self.gq, self.sT, self.sV, self.sS, self.sG
        V, SC, T, SY, G = nc.vector, nc.scalar, nc.tensor, nc.sync, nc.gpsimd

        self.xatt = self.view([P, TC, D], F32, 64)
        wtile = [self.view([P, 3, 8, 128], BF16, 160),
                 self.view([P, 3, 8, 128], BF16, 166)]
        xs = self.view([P, S], F32, 172)
        ah = self.view([P, S], F32, 180)
        t1 = self.view([P, 512], F32, 188)
        t2 = self.view([P, 512], F32, 190)

        with contextlib.ExitStack() as pes:
            pgvd = [pes.enter_context(nc.psum_tensor(f"pC{i}", [P, 512], F32))
                    for i in range(6)]
            ph = [pes.enter_context(nc.psum_tensor(f"pH{i}", [P, 512], F32))
                  for i in range(2)]
            xnb_done = sS.n
            gvd_free = [None, None]
            ph_free = [None, None]
            wfree = [None, None]
            h_consumed_t = None
            ep_v_last = None

            for jc in range(8):
                wb = jc % 2
                if wfree[wb] is not None:
                    G.wait_ge(sT.h, wfree[wb])
                for wi, wn in enumerate(("Wg", "Wv", "Wd")):
                    gq.inc(G.dma_start(
                        out=wtile[wb][:, wi, :, :],
                        in_=dram[wn][:, jc * 128:(jc + 1) * 128]
                            .rearrange("(a p) j -> p a j", p=P)), 16)
                w_ready = gq.n

                T.wait_ge(gq.h, w_ready)
                for sl in range(4):
                    pset = sl % 2
                    if gvd_free[pset] is not None:
                        T.wait_ge(gvd_free[pset][0].h, gvd_free[pset][1])
                    for wi in range(3):
                        pp = pgvd[pset * 3 + wi]
                        for ic in range(8):
                            i = T.matmul(pp.ap(), wtile[wb][:, wi, ic, :],
                                         self.xnb[:, ic, sl * 512:(sl + 1) * 512],
                                         start=(ic == 0), stop=(ic == 7))
                    sT.inc(i)
                    mm_done = sT.n

                    SC.wait_ge(sT.h, mm_done)
                    if sl == 0 and h_consumed_t is not None:
                        SC.wait_ge(sT.h, h_consumed_t)
                    if ep_v_last is not None:
                        SC.wait_ge(sV.h, ep_v_last)
                    SC.activation(t1.ap(), pgvd[pset * 3 + 0].ap(), AF.Sigmoid,
                                  bias=self.bg_c[:, jc:jc + 1])
                    SC.activation(t2.ap(), pgvd[pset * 3 + 1].ap(), AF.Tanh,
                                  bias=self.bv_c[:, jc:jc + 1])
                    sS.inc(SC.activation(ah[:, sl * 512:(sl + 1) * 512],
                                         pgvd[pset * 3 + 2].ap(), AF.Sigmoid,
                                         bias=self.bd_c[:, jc:jc + 1]))
                    ep_s = sS.n
                    V.wait_ge(sS.h, ep_s)
                    V.tensor_tensor(xs[:, sl * 512:(sl + 1) * 512], t1.ap(),
                                    t2.ap(), ALU.mult)
                    sV.inc(V.tensor_scalar(ah[:, sl * 512:(sl + 1) * 512],
                                           ah[:, sl * 512:(sl + 1) * 512],
                                           0.998, 0.001, ALU.mult, ALU.add))
                    gvd_free[pset] = (sV, sV.n)
                    ep_v_last = sV.n
                sV.inc(V.tensor_tensor_scan(ah.ap(), ah.ap(), xs.ap(), 0.0,
                                            ALU.mult, ALU.add))
                scan_done = sV.n
                T.wait_ge(sV.h, scan_done)
                for tq in range(4):
                    b = tq % 2
                    if ph_free[b] is not None:
                        T.wait_ge(ph_free[b][0].h, ph_free[b][1])
                    for k in range(4):
                        tc = tq * 4 + k
                        i = T.transpose(ph[b][:, k * 128:(k + 1) * 128],
                                        ah[:, tc * 128:(tc + 1) * 128],
                                        self.identf.ap())
                    sT.inc(i)
                    tcnt = sT.n
                    V.wait_ge(sT.h, tcnt)
                    sV.inc(V.tensor_tensor(
                        self.xatt[:, tq * 4:tq * 4 + 4, jc * 128:(jc + 1) * 128],
                        self.xn_tm[:, tq * 4:tq * 4 + 4, jc * 128:(jc + 1) * 128],
                        ph[b].ap().rearrange("p (a t) -> p a t", a=4), ALU.add))
                    ph_free[b] = (sV, sV.n)
                h_consumed_t = sT.n
                wfree[wb] = mm_done

        if "xatt" in self.taps:
            SY.wait_ge(sV.h, sV.n)
            sq.inc(SY.dma_start(out=self.taps["xatt"],
                                in_=self.xatt.ap().rearrange("p a d -> p (a d)")), 16)

    def phase_C2pre(self):
        nc, st, dram = self.nc, self.st, self.dram
        sq, sT, sV, sS = self.sq, self.sT, self.sV, self.sS
        V, SC, T, SY = nc.vector, nc.scalar, nc.tensor, nc.sync
        self.ynb = self.view([P, 8, S], BF16, 128)
        x2b_q = self.view([P, 4, D], BF16, 168)
        sqscr = self.view([P, D], F32, 176)
        self.c2_wgt = [self.view([P, 8, 128], BF16, 160), self.view([P, 8, 128], BF16, 162)]
        self.c2_wut = [self.view([P, 8, 128], BF16, 164), self.view([P, 8, 128], BF16, 166)]
        self.c2_w_ready = [None, None]
        gq = self.gq
        G = nc.gpsimd
        for c in range(2):
            gq.inc(G.dma_start(out=self.c2_wgt[c].ap(),
                               in_=dram["Wgate"][:, c * 128:(c + 1) * 128]
                                   .rearrange("(a p) f -> p a f", p=P)), 16)
            gq.inc(G.dma_start(out=self.c2_wut[c].ap(),
                               in_=dram["Wup"][:, c * 128:(c + 1) * 128]
                                   .rearrange("(a p) f -> p a f", p=P)), 16)
            self.c2_w_ready[c] = gq.n

        for tc in range(TC):
            i = SC.activation(sqscr.ap(), self.xatt[:, tc, :], AF.Square,
                              accum_out=st["s2f"][:, tc:tc + 1])
        SC.activation(st["t1"].ap(), st["s2f"].ap(), AF.Ln, bias=1e-6, scale=1.0 / D)
        self.sg()
        SC.activation(st["r_ffn"].ap(), st["t1"].ap(), AF.Exp, scale=-0.5)
        self.sg(sS)
        rffn_done = sS.n

        with contextlib.ExitStack() as pes:
            pb = [pes.enter_context(nc.psum_tensor(f"pY{i}", [P, 512], BF16))
                  for i in range(2)]
            bank_free = [None, None]
            q_trans_done = None
            for q in range(4):
                SC.wait_ge(sS.h, rffn_done)
                if q_trans_done is not None:
                    SC.wait_ge(sT.h, q_trans_done)
                for k in range(4):
                    tc = q * 4 + k
                    i = SC.activation(x2b_q[:, k, :], self.xatt[:, tc, :],
                                      AF.Identity, scale=st["r_ffn"][:, tc:tc + 1])
                sS.inc(i)
                x2b_done = sS.n
                T.wait_ge(sS.h, x2b_done)
                for fc in range(8):
                    b = fc % 2
                    if bank_free[b] is not None:
                        T.wait_ge(bank_free[b][0].h, bank_free[b][1])
                    for k in range(4):
                        i = T.transpose(pb[b][:, k * 128:(k + 1) * 128],
                                        x2b_q[:, k, fc * 128:(fc + 1) * 128],
                                        self.ident_b.ap())
                    sT.inc(i)
                    SC.wait_ge(sT.h, sT.n)
                    sS.inc(SC.activation(self.ynb[:, fc, q * 512:(q + 1) * 512],
                                         pb[b].ap(), AF.Identity,
                                         scale=self.wrffn_c[:, fc:fc + 1]))
                    bank_free[b] = (sS, sS.n)
                q_trans_done = sT.n

    def phase_C2(self):
        nc, st, dram = self.nc, self.st, self.dram
        sq, gq, sT, sV, sS = self.sq, self.gq, self.sT, self.sV, self.sS
        V, SC, T, SY, G = nc.vector, nc.scalar, nc.tensor, nc.sync, nc.gpsimd
        p1slabs = [self.view([P, 1024], F32, 172), self.view([P, 1024], F32, 176)]
        p1junk = self.view([P, 1024], F32, 180)
        bg_fwg = self.p1_gen("fwg", p1slabs, p1junk, ["S"], 1024)

        H1 = self.view([P, 8, S], BF16, 0)
        woutT = [self.view([P, 8, D], BF16, 32), self.view([P, 8, D], BF16, 48)]
        wgt = self.c2_wgt
        wut = self.c2_wut
        t1 = [self.view([P, 512], F32, 168), self.view([P, 512], F32, 170)]

        with contextlib.ExitStack() as pes:
            pG = [pes.enter_context(nc.psum_tensor(f"pG{i}", [P, 512], F32))
                  for i in range(2)]
            pU = [pes.enter_context(nc.psum_tensor(f"pU{i}", [P, 512], F32))
                  for i in range(2)]
            pO = [pes.enter_context(nc.psum_tensor(f"pO{i}", [P, 1024], F32))
                  for i in range(2)]
            pGU_free = [None, None]
            pO_free = [None, None]
            w_free = [None, None]
            wo_free = [None, None]
            h1_ready_after_third = None

            for cb in range(3):
                wob = cb % 2
                if wo_free[wob] is not None:
                    G.wait_ge(sT.h, wo_free[wob])
                gq.inc(G.dma_start(
                    out=woutT[wob].ap(),
                    in_=dram["Wout"][cb * 1024:(cb + 1) * 1024, :]
                        .rearrange("(a p) j -> p a j", p=P)), 16)
                wout_ready = gq.n
                for c8 in range(8):
                    c = cb * 8 + c8
                    wb = c8 % 2
                    if c < 2:
                        w_ready = self.c2_w_ready[c]
                    else:
                        if w_free[wb] is not None:
                            G.wait_ge(sT.h, w_free[wb])
                        gq.inc(G.dma_start(
                            out=wgt[wb].ap(),
                            in_=dram["Wgate"][:, c * 128:(c + 1) * 128]
                                .rearrange("(a p) f -> p a f", p=P)), 16)
                        gq.inc(G.dma_start(
                            out=wut[wb].ap(),
                            in_=dram["Wup"][:, c * 128:(c + 1) * 128]
                                .rearrange("(a p) f -> p a f", p=P)), 16)
                        w_ready = gq.n
                    T.wait_ge(gq.h, w_ready)
                    for sl in range(4):
                        ps = sl % 2
                        if pGU_free[ps] is not None:
                            T.wait_ge(pGU_free[ps][0].h, pGU_free[ps][1])
                        for ic in range(8):
                            T.matmul(pG[ps].ap(), wgt[wb][:, ic, :],
                                     self.ynb[:, ic, sl * 512:(sl + 1) * 512],
                                     start=(ic == 0), stop=(ic == 7))
                        for ic in range(8):
                            i = T.matmul(pU[ps].ap(), wut[wb][:, ic, :],
                                         self.ynb[:, ic, sl * 512:(sl + 1) * 512],
                                         start=(ic == 0), stop=(ic == 7))
                        sT.inc(i)
                        mm_done = sT.n
                        SC.wait_ge(sT.h, mm_done)
                        sS.inc(SC.activation(t1[ps].ap(), pG[ps].ap(), AF.Silu))
                        sil_done = sS.n
                        V.wait_ge(sS.h, sil_done)
                        sV.inc(V.tensor_tensor(H1[:, c8, sl * 512:(sl + 1) * 512],
                                               t1[ps].ap(), pU[ps].ap(), ALU.mult))
                        pGU_free[ps] = (sV, sV.n)
                    w_free[wb] = sT.n
                    self.pump(bg_fwg)
                h1_done = sV.n
                T.wait_ge(sV.h, h1_done)
                T.wait_ge(gq.h, wout_ready)
                for tc in range(TC):
                    po = tc % 2
                    if pO_free[po] is not None:
                        T.wait_ge(pO_free[po][0].h, pO_free[po][1])
                    for c8 in range(8):
                        for half in range(2):
                            i = T.matmul(pO[po][:, half * 512:(half + 1) * 512],
                                         H1[:, c8, tc * 128:(tc + 1) * 128],
                                         woutT[wob][:, c8, half * 512:(half + 1) * 512],
                                         start=(c8 == 0), stop=(c8 == 7))
                    sT.inc(i)
                    ob_done = sT.n
                    V.wait_ge(sT.h, ob_done)
                    sV.inc(V.tensor_tensor(self.xatt[:, tc, :],
                                           self.xatt[:, tc, :], pO[po].ap(),
                                           ALU.add))
                    pO_free[po] = (sV, sV.n)
                    self.pump(bg_fwg)
                wo_free[wob] = sT.n

        self.drain(bg_fwg)
        SY.wait_ge(sV.h, sV.n)
        sq.inc(SY.dma_start(out=dram["s_sp"].rearrange("(a p) d -> p a d", p=P),
                            in_=self.xatt.ap()), 16)
        self.s_spilled = sq.n
        if "s" in self.taps:
            sq.inc(SY.dma_start(out=self.taps["s"].rearrange("(a p) d -> p a d", p=P),
                                in_=self.xatt.ap()), 16)

    def phase_D0(self):
        nc, st, dram = self.nc, self.st, self.dram
        sq, gq, sT, sV, sS, sG = self.sq, self.gq, self.sT, self.sV, self.sS, self.sG
        V, SC, T, SY, G = nc.vector, nc.scalar, nc.tensor, nc.sync, nc.gpsimd
        wq = self.wq

        psum_scope = contextlib.ExitStack()
        pst = psum_scope.enter_context(nc.psum_tensor("pst", [P, 512], F32))

        def thresholds(wn, mw, thr, nthr):
            f, nsl = self.p1_state[wn]
            T.wait_ge(f[0][0].h, f[0][1])
            T.wait_ge(f[1][0].h, f[1][1])
            T.wait_ge(sG.h, self.idents_ready)
            T.wait_ge(sV.h, sV.n)
            sT.inc(T.matmul(pst[:, 0:nsl], self.ones128.ap(),
                            self.p1acc[wn][:, 0:nsl], start=True, stop=True))
            V.wait_ge(sT.h, sT.n)
            V.reduce_sum(wq[mw].ap(), pst[:, 0:nsl], AX.X)
            self.vg()
            V.tensor_scalar(wq[mw].ap(), wq[mw].ap(), 1.0 / (II * D), 1e-5,
                            ALU.mult, ALU.max)
            self.vg()
            V.tensor_scalar(wq[thr].ap(), wq[mw].ap(), 0.5, None, ALU.mult)
            self.vg()
            V.tensor_scalar(wq[nthr].ap(), wq[thr].ap(), -1.0, None, ALU.mult)
            self.vg(sV)

        thresholds("fwg", "mwg", "thrg", "nthrg")
        fwu_slabs = [self.view([P, 8192], F32, 64), self.view([P, 8192], F32, 96)]
        fwu_junk = self.view([P, 8192], F32, 128)
        bg_fwu = self.p1_gen("fwu", fwu_slabs, fwu_junk, ["S"], 8192)
        fwd_slabs = [self.view([P, 4096], F32, 128), self.view([P, 4096], F32, 144)]
        bg_fwd = self.p1_gen("fwd", fwd_slabs, None, ["V"], 4096)
        self.qwgT = self.view([P, 8, II], BF16, 0)
        self.qwuT = self.view([P, 8, II], BF16, 64)
        wsl = [self.view([P, 2, D], F32, 160), self.view([P, 2, D], F32, 168)]
        ctmp = self.view([P, 2 * D], F32, 176)
        qsl = self.view([P, 2, D], BF16, 184)
        with contextlib.ExitStack() as pes:
            pb = [pes.enter_context(nc.psum_tensor(f"pW{i}", [P, 512], BF16))
                  for i in range(2)]
            slab_free = [None, None]
            bank_free = [None, None]
            last_trans = None
            for mi, (wn, dst, thr, nthr) in enumerate((
                    ("fwg", self.qwgT, "thrg", "nthrg"),
                    ("fwu", self.qwuT, "thru", "nthru"))):
                for sl in range(16):
                    sb_ = sl % 2
                    if slab_free[sb_] is not None:
                        SY.wait_ge(sV.h, slab_free[sb_])
                    sq.inc(SY.dma_start(
                        out=wsl[sb_].ap(),
                        in_=dram[wn].rearrange("(s a p) j -> s p a j", a=2, p=P)[sl]), 16)
                    dma_n = sq.n
                    V.wait_ge(sq.h, dma_n)
                    flat = wsl[sb_].ap().rearrange("p a j -> p (a j)")
                    V.tensor_scalar(ctmp.ap(), flat, wq[thr].ap(), None, ALU.is_gt)
                    V.tensor_scalar(flat, flat, wq[nthr].ap(), None, ALU.is_lt)
                    if last_trans is not None:
                        V.wait_ge(sT.h, last_trans)
                    sV.inc(V.tensor_tensor(qsl.ap().rearrange("p a j -> p (a j)"),
                                           ctmp.ap(), flat, ALU.subtract))
                    q_done = sV.n
                    slab_free[sb_] = q_done
                    slab_free[sb_] = q_done
                    T.wait_ge(sV.h, q_done)
                    for jc in range(8):
                        b = jc % 2
                        if bank_free[b] is not None:
                            T.wait_ge(bank_free[b][0].h, bank_free[b][1])
                        for a in range(2):
                            i = T.transpose(pb[b][:, a * 128:(a + 1) * 128],
                                            qsl[:, a, jc * 128:(jc + 1) * 128],
                                            self.ident_b.ap())
                        sT.inc(i)
                        SC.wait_ge(sT.h, sT.n)
                        sS.inc(SC.copy(dst[:, jc, sl * 256:(sl + 1) * 256],
                                       pb[b][:, 0:256]))
                        bank_free[b] = (sS, sS.n)
                    last_trans = sT.n
                    self.pump(bg_fwu if mi == 0 else bg_fwd)
                if mi == 0:
                    self.drain(bg_fwu)
                    thresholds("fwu", "mwu", "thru", "nthru")
            self.drain(bg_fwd)
            thresholds("fwd", "mwd", "thrd", "nthrd")
        psum_scope.close()

        V.tensor_tensor(st["t0"].ap(), st["sxinv"].ap(), st["sxinv"].ap(), ALU.mult)
        self.vg()
        V.tensor_tensor(st["kq"].ap(), st["t0"].ap(), st["sxinv"].ap(), ALU.mult)
        self.vg()
        V.tensor_scalar(st["kq"].ap(), st["kq"].ap(), wq["mwg"].ap(), None, ALU.mult)
        self.vg()
        V.tensor_scalar(st["kq"].ap(), st["kq"].ap(), wq["mwg"].ap(), None, ALU.mult)
        self.vg()
        V.tensor_scalar(st["kq"].ap(), st["kq"].ap(), wq["mwu"].ap(), None, ALU.mult)
        self.vg(sV)
        if "qwgT" in self.taps:
            SY.wait_ge(sV.h, sV.n)
            SY.wait_ge(sS.h, sS.n)
            sq.inc(SY.dma_start(out=self.taps["qwgT"].rearrange("(a p) i -> p a i", p=P),
                                in_=self.qwgT.ap()), 16)
        if "stats" in self.taps:
            SY.wait_ge(sV.h, sV.n)
            SY.wait_ge(sS.h, sS.n)
            sq.inc(SY.dma_start(out=self.taps["stats"][:, 16:24], in_=acc["fwg"].ap()), 16)
            sq.inc(SY.dma_start(out=self.taps["stats"][:, 24:32], in_=acc["fwd"].ap()), 16)
            for i2, nm in enumerate(("mwg", "thrg", "mwu", "thru", "mwd", "thrd", "sabs")):
                sq.inc(SY.dma_start(out=self.taps["stats"][:, i2:i2 + 1],
                                    in_=wq[nm].ap(),
                                    allow_slow_non_contiguous=True), 16)

    SERIAL_D1 = False

    def phase_D1(self):
        nc, st, dram = self.nc, self.st, self.dram
        sq, sT, sV, sS = self.sq, self.sT, self.sV, self.sS
        gq, sG = self.gq, self.sG
        V, SC, T, SY, G = nc.vector, nc.scalar, nc.tensor, nc.sync, nc.gpsimd
        wq = self.wq

        qx_t = [self.view([P, 8, 128], BF16, 160), self.view([P, 8, 128], BF16, 162)]
        qhT = self.view([P, 32, 128], BF16, 176)
        qh_tm = self.view([P, II], BF16, 184)
        g_tm = self.view([P, II], F32, 128)
        h_tm = self.view([P, II], F32, 144)
        qx_used = [None, None]
        qx_ready = [None, None]
        fsl = self.view([P, 1024], F32, 164)
        fctmp = self.view([P, 1024], F32, 168)
        fqsl = self.view([P, 1024], BF16, 172)
        fwstage = self.view([P, 8, 128], BF16, 174)

        with contextlib.ExitStack() as pes:
            pQ = [pes.enter_context(nc.psum_tensor(f"pQ{i}", [P, 1024], F32))
                  for i in range(3)]
            pqh = pes.enter_context(nc.psum_tensor("pqh", [P, 1024], BF16))
            pwd = pes.enter_context(nc.psum_tensor("pwd", [P, 1024], BF16))

            fstate = {"q_done": None, "store": None, "trans": None}

            def fwd_unit(u):
                a, iq = u % 8, u // 8
                if fstate["q_done"] is not None:
                    G.wait_ge(sG.h, fstate["q_done"])
                gq.inc(G.dma_start(
                    out=fsl.ap(),
                    in_=dram["fwd"][a * 128:(a + 1) * 128,
                                    iq * 1024:(iq + 1) * 1024]
                        .rearrange("(o p) i -> p o i", p=P)[:, 0, :]), 16)
                dma_n = gq.n
                G.wait_ge(gq.h, dma_n)
                G.tensor_scalar(fctmp.ap(), fsl.ap(), wq["thrd"].ap(), None,
                                ALU.is_gt)
                G.tensor_scalar(fsl.ap(), fsl.ap(), wq["nthrd"].ap(), None,
                                ALU.is_lt)
                if fstate["trans"] is not None:
                    G.wait_ge(sT.h, fstate["trans"])
                sG.inc(G.tensor_tensor(fqsl.ap(), fctmp.ap(), fsl.ap(),
                                       ALU.subtract))
                fstate["q_done"] = sG.n
                T.wait_ge(sG.h, fstate["q_done"])
                for k in range(8):
                    i2 = T.transpose(pwd[:, k * 128:(k + 1) * 128],
                                     fqsl[:, k * 128:(k + 1) * 128],
                                     self.ident_b.ap())
                sT.inc(i2)
                fstate["trans"] = sT.n
                if fstate["store"] is not None:
                    SC.wait_ge(gq.h, fstate["store"])
                SC.wait_ge(sT.h, fstate["trans"])
                sS.inc(SC.copy(fwstage.ap(),
                               pwd.ap().rearrange("p (k t) -> p k t", k=8)))
                ev_n = sS.n
                G.wait_ge(sS.h, ev_n)
                gq.inc(G.dma_start(
                    out=dram["qwd_sp"][iq, :, :, a, :].rearrange("k p h -> p k h"),
                    in_=fwstage.ap()), 16)
                fstate["store"] = gq.n
            pQ_free = [None, None, None]
            pqh_free = None
            stage_free = None
            trans_done = None
            pending = None
            prev_qh_done = None
            rot = [0]
            for tc in range(TC):
                qb = tc % 2
                if qx_used[qb] is not None:
                    SY.wait_ge(sT.h, qx_used[qb])
                sq.inc(SY.dma_start(
                    out=qx_t[qb].ap(),
                    in_=dram["qx_sp"].rearrange("(a p) t -> p a t", p=P)[:, :, tc * 128:(tc + 1) * 128]), 16)
                qx_ready[qb] = sq.n
                T.wait_ge(sq.h, qx_ready[qb])
                g_evac_done = [None] * 4
                sq_done_q = [None] * 4
                if prev_qh_done is not None:
                    SC.wait_ge(sV.h, prev_qh_done)
                for qq in range(4):
                    ps = qq % 3
                    if pQ_free[ps] is not None:
                        T.wait_ge(pQ_free[ps][0].h, pQ_free[ps][1])
                    for j in range(8):
                        for nb in range(2):
                            i = T.matmul(pQ[ps][:, nb * 512:(nb + 1) * 512],
                                         qx_t[qb][:, j, :],
                                         self.qwgT[:, j, qq * 1024 + nb * 512:
                                                   qq * 1024 + (nb + 1) * 512],
                                         start=(j == 0), stop=(j == 7))
                    sT.inc(i)
                    SC.wait_ge(sT.h, sT.n)
                    sS.inc(SC.activation(g_tm[:, qq * 1024:(qq + 1) * 1024],
                                         pQ[ps].ap(), AF.Relu))
                    g_evac_done[qq] = sS.n
                    pQ_free[ps] = (sS, sS.n)
                g_done = sV.n
                if "g4" in self.taps and tc == 1:
                    SY.wait_ge(sV.h, g_done)
                    sq.inc(SY.dma_start(out=self.taps["g4"], in_=g_tm.ap()), 16)
                for qq in range(4):
                    sS.inc(SC.activation(h_tm[:, qq * 1024:(qq + 1) * 1024],
                                         g_tm[:, qq * 1024:(qq + 1) * 1024],
                                         AF.Square))
                    sq_done_q[qq] = sS.n
                for qq in range(4):
                    ps = qq % 3
                    if pQ_free[ps] is not None:
                        T.wait_ge(pQ_free[ps][0].h, pQ_free[ps][1])
                    for j in range(8):
                        for nb in range(2):
                            i = T.matmul(pQ[ps][:, nb * 512:(nb + 1) * 512],
                                         qx_t[qb][:, j, :],
                                         self.qwuT[:, j, qq * 1024 + nb * 512:
                                                   qq * 1024 + (nb + 1) * 512],
                                         start=(j == 0), stop=(j == 7))
                    sT.inc(i)
                    V.wait_ge(sT.h, sT.n)
                    V.wait_ge(sS.h, sq_done_q[qq])
                    sV.inc(V.tensor_tensor(h_tm[:, qq * 1024:(qq + 1) * 1024],
                                           h_tm[:, qq * 1024:(qq + 1) * 1024],
                                           pQ[ps].ap(), ALU.mult))
                    pQ_free[ps] = (sV, sV.n)
                qx_used[qb] = sT.n
                if pending is not None:
                    emit_transposes(*pending)
                    pending = None
                V.reduce_max(st["amaxh"][:, tc:tc + 1], h_tm.ap(), AX.X,
                             apply_absolute_value=True)
                self.vg()
                V.tensor_tensor(st["t0"][:, tc:tc + 1], st["kq"][:, tc:tc + 1],
                                st["amaxh"][:, tc:tc + 1], ALU.mult)
                self.vg()
                V.tensor_scalar(st["clipd"][:, tc:tc + 1],
                                st["t0"][:, tc:tc + 1], 1e-5, None, ALU.max)
                self.vg(sV)
                clip_done = sV.n
                SC.wait_ge(sV.h, clip_done)
                SC.activation(st["t1"][:, tc:tc + 1], st["clipd"][:, tc:tc + 1], AF.Ln)
                self.sg()
                SC.activation(st["t2"][:, tc:tc + 1], st["t1"][:, tc:tc + 1],
                              AF.Exp, scale=-1.0)
                self.sg(sS)
                sh_done = sS.n
                V.wait_ge(sS.h, sh_done)
                V.tensor_scalar(st["t2"][:, tc:tc + 1], st["t2"][:, tc:tc + 1],
                                127.0, None, ALU.mult)
                self.vg()
                V.tensor_tensor(st["m2"][:, tc:tc + 1], st["kq"][:, tc:tc + 1],
                                st["t2"][:, tc:tc + 1], ALU.mult)
                self.vg()
                V.tensor_scalar(st["gsc"][:, tc:tc + 1], st["clipd"][:, tc:tc + 1],
                                1.0 / 127.0, None, ALU.mult)
                self.vg()
                V.tensor_scalar(st["gsc"][:, tc:tc + 1],
                                st["gsc"][:, tc:tc + 1], wq["mwd"].ap(),
                                None, ALU.mult)
                self.vg(sV)
                if trans_done is not None:
                    V.wait_ge(sT.h, trans_done)
                V.tensor_scalar(g_tm.ap(), h_tm.ap(), st["m2"][:, tc:tc + 1],
                                MAGIC, ALU.mult, ALU.add)
                sV.inc(V.tensor_scalar(qh_tm.ap(), g_tm.ap(), -MAGIC, None, ALU.add))
                qh_done = sV.n
                prev_qh_done = qh_done
                pend = (tc, qh_done)
                if "h4" in self.taps and tc == 1:
                    SY.wait_ge(sV.h, qh_done)
                    sq.inc(SY.dma_start(out=self.taps["h4"], in_=h_tm.ap()), 16)

                def emit_transposes(ptc, pqh_done):
                    nonlocal pqh_free, stage_free, trans_done
                    T.wait_ge(sV.h, pqh_done)
                    if stage_free is not None:
                        T.wait_ge(sq.h, stage_free)
                    for grp in range(4):
                        if pqh_free is not None:
                            T.wait_ge(pqh_free[0].h, pqh_free[1])
                        for k in range(8):
                            ib = grp * 8 + k
                            i = T.transpose(pqh[:, k * 128:(k + 1) * 128],
                                            qh_tm[:, ib * 128:(ib + 1) * 128],
                                            self.ident_b.ap())
                        sT.inc(i)
                        V.wait_ge(sT.h, sT.n)
                        sV.inc(V.tensor_copy(
                            qhT[:, grp * 8:(grp + 1) * 8, :],
                            pqh.ap().rearrange("p (k t) -> p k t", k=8)))
                        pqh_free = (sV, sV.n)
                    trans_done = sT.n
                    SY.wait_ge(sV.h, sV.n)
                    sq.inc(SY.dma_start(
                        out=dram["qh_sp"][ptc].rearrange("a p t -> p a t"),
                        in_=qhT.ap()), 16)
                    stage_free = sq.n
                fwd_unit(2 * tc)
                fwd_unit(2 * tc + 1)
                if tc == TC - 1:
                    emit_transposes(*pend)
                    pending = None
                    qwdT_new = self.view([P, 32, D], BF16, 0)
                    SY.wait_ge(sT.h, qx_used[qb])
                    SY.wait_ge(gq.h, fstate["store"])
                    sq.inc(SY.dma_start(
                        out=qwdT_new.ap(),
                        in_=dram["qwd_sp"].rearrange("q k p a h -> p (q k) (a h)")), 16)
                    self.qwdT = qwdT_new
                    self.qwdT_ready = sq.n
                else:
                    pending = pend
        if "qh" in self.taps:
            SY.wait_ge(sq.h, sq.n)
            sq.inc(SY.dma_start(out=self.taps["qh"], in_=dram["qh_sp"]), 16)
        if "stats" in self.taps:
            SY.wait_ge(sV.h, sV.n)
            SY.wait_ge(sS.h, sS.n)
            for i2, nm in enumerate(("amaxh", "kq", "m2", "gsc", "clipd", "t2")):
                sq.inc(SY.dma_start(
                    out=self.taps["stats"][:, 160 + i2 * 16:160 + (i2 + 1) * 16],
                    in_=st[nm].ap()), 16)

    def phase_D2(self):
        nc, st, dram = self.nc, self.st, self.dram
        sq, sT, sV, sS = self.sq, self.sT, self.sV, self.sS
        V, SC, T, SY = nc.vector, nc.scalar, nc.tensor, nc.sync
        wq = self.wq

        T.wait_ge(sq.h, self.qwdT_ready)
        SY.wait_ge(sV.h, sV.n)
        SY.wait_ge(sS.h, sS.n)
        SY.wait_ge(sT.h, sT.n)
        NB = 3
        qhT_t = [self.view([P, 32, 128], BF16, 96 + 8 * i) for i in range(NB)]
        s_t = [self.view([P, D], F32, 120 + 4 * i) for i in range(NB)]
        x_t = [self.view([P, D], F32, 132 + 4 * i) for i in range(NB)]
        gm_t = [self.view([P, D], F32, 160), self.view([P, D], F32, 164)]
        with contextlib.ExitStack() as pes:
            pd = [pes.enter_context(nc.psum_tensor(f"pD{i}", [P, 1024], F32))
                  for i in range(2)]
            pd_free = [None, None]
            in_free = [None] * NB
            in_ready = [None] * NB
            gm_free = [None, None]
            out_done = [None] * TC

            def issue_inputs(tc):
                bb = tc % NB
                if in_free[bb] is not None:
                    SY.wait_ge(sV.h, in_free[bb])
                sq.inc(SY.dma_start(out=qhT_t[bb].ap(),
                                    in_=dram["qh_sp"][tc].rearrange("a p t -> p a t")), 16)
                sq.inc(SY.dma_start(out=s_t[bb].ap(),
                                    in_=dram["s_sp"].rearrange("(a p) d -> p a d", p=P)[:, tc, :]), 16)
                sq.inc(SY.dma_start(out=x_t[bb].ap(),
                                    in_=dram["x"].rearrange("(a p) d -> p a d", p=P)[:, tc, :]), 16)
                in_ready[bb] = sq.n

            for tc in range(min(2, TC)):
                issue_inputs(tc)
            for tc in range(TC):
                bb = tc % NB
                po = tc % 2
                if pd_free[po] is not None:
                    T.wait_ge(pd_free[po][0].h, pd_free[po][1])
                T.wait_ge(sq.h, in_ready[bb])
                for ib in range(32):
                    for half in range(2):
                        i = T.matmul(pd[po][:, half * 512:(half + 1) * 512],
                                     qhT_t[bb][:, ib, :],
                                     self.qwdT[:, ib, half * 512:(half + 1) * 512],
                                     start=(ib == 0), stop=(ib == 31))
                sT.inc(i)
                dn_done = sT.n
                if gm_free[bb % 2] is not None:
                    SC.wait_ge(sq.h, gm_free[bb % 2])
                SC.wait_ge(sT.h, dn_done)
                sS.inc(SC.activation(gm_t[bb % 2].ap(), pd[po].ap(), AF.Silu,
                                     scale=st["gsc"][:, tc:tc + 1]))
                pd_free[po] = (sS, sS.n)
                gm_done = sS.n
                V.wait_ge(sS.h, gm_done)
                V.tensor_tensor(gm_t[bb % 2].ap(), gm_t[bb % 2].ap(), s_t[bb].ap(),
                                ALU.mult)
                sV.inc(V.tensor_tensor(gm_t[bb % 2].ap(), gm_t[bb % 2].ap(),
                                       x_t[bb].ap(), ALU.add))
                out_done[tc] = sV.n
                in_free[bb] = out_done[tc]
                if tc + 2 < TC:
                    issue_inputs(tc + 2)
                SY.wait_ge(sV.h, out_done[tc])
                sq.inc(SY.dma_start(
                    out=dram["y"].rearrange("(a p) d -> p a d", p=P)[:, tc, :],
                    in_=gm_t[bb % 2].ap()), 16)
                gm_free[bb % 2] = sq.n
        SY.wait_ge(sq.h, sq.n)


def build(debug=(), stop_after=None):
    return KBuild(debug, stop_after).build()


_NC_CACHE = {}


def _build_nc():
    if "nc" not in _NC_CACHE:
        _NC_CACHE["nc"] = KBuild().build()
    return _NC_CACHE["nc"]


def kernel(L0_attn_x, L0_attn_w_rms_mix, L0_attn_w_rms_ffn, L0_attn_Wg,
           L0_attn_bg, L0_attn_Wv, L0_attn_bv, L0_attn_Wd, L0_attn_bd,
           L0_attn_W_gate, L0_attn_W_up, L0_attn_W_out,
           L0_ffn_w_gate, L0_ffn_w_up, L0_ffn_w_down):
    import numpy as np
    from concourse.bass_utils import run_bass_kernel_spmd

    f32 = lambda a: np.ascontiguousarray(np.asarray(a, dtype=np.float32))
    shared = {
        "wrmix": f32(L0_attn_w_rms_mix), "wrffn": f32(L0_attn_w_rms_ffn),
        "Wg": f32(L0_attn_Wg), "Wv": f32(L0_attn_Wv), "Wd": f32(L0_attn_Wd),
        "bg": f32(L0_attn_bg), "bv": f32(L0_attn_bv), "bd": f32(L0_attn_bd),
        "Wgate": f32(L0_attn_W_gate), "Wup": f32(L0_attn_W_up),
        "Wout": f32(L0_attn_W_out),
        "fwg": f32(L0_ffn_w_gate), "fwu": f32(L0_ffn_w_up),
        "fwd": f32(L0_ffn_w_down),
    }
    x = np.asarray(L0_attn_x, dtype=np.float32)
    in_maps = [dict(shared, x=np.ascontiguousarray(x[b])) for b in range(8)]
    nc = _build_nc()
    res = run_bass_kernel_spmd(nc, in_maps, core_ids=list(range(8)))
    return np.stack([res.results[b]["y"] for b in range(8)], axis=0)


# version 12
# speedup vs baseline: 1.2211x; 1.2211x over previous
import contextlib
import math

import concourse.bass as bass
import concourse.mybir as mybir

F32 = mybir.dt.float32
BF16 = mybir.dt.bfloat16
AF = mybir.ActivationFunctionType
ALU = mybir.AluOpType
AX = mybir.AxisListType

MAGIC = float(3 * 2.0 ** 22)
S, D, FF, II = 2048, 1024, 3072, 4096
TC, P = 16, 128
KB = 1024


class Sem:
    def __init__(self, es, nc, name):
        self.h = es.enter_context(nc.semaphore(name))
        self.n = 0

    def inc(self, instr, k=1):
        instr.then_inc(self.h, k)
        self.n += k
        return instr


class KBuild:
    def __init__(self, debug=(), stop_after=None):
        self.debug = set(debug)
        self.stop_after = stop_after

    def build(self):
        es = contextlib.ExitStack()
        self.es = es
        nc = bass.Bass("TRN2", target_bir_lowering=False,
                       detect_race_conditions=False)
        self.nc = nc
        with es:
            self.declare()
            self.blk = es.enter_context(nc.Block())
            nc.gpsimd.dma_reset()
            for s in self.all_sems:
                nc.gpsimd.sem_clear(s.h)
            nc.vector.memset(self.ceps[:, 0:1], 1e-5)
            nc.vector.memset(self.ceps[:, 1:2], 1e-6)
            nc.all_engine_barrier()
            phases = [("A", self.phase_A), ("C1", self.phase_C1),
                      ("C2pre", self.phase_C2pre), ("C2", self.phase_C2),
                      ("D0", self.phase_D0), ("D1", self.phase_D1),
                      ("D2", self.phase_D2)]
            for name, fn in phases:
                fn()
                nc.all_engine_barrier()
                if self.stop_after == name:
                    break
            nc.sync.wait_ge(self.sq.h, self.sq.n)
            nc.gpsimd.wait_ge(self.gq.h, self.gq.n)
        return nc

    def declare(self):
        nc, es = self.nc, self.es
        dt = {}
        self.dram = dt
        ein = lambda n, s: nc.dram_tensor(n, s, F32, kind="ExternalInput").ap()
        dt["x"] = ein("x", [S, D])
        dt["wrmix"] = ein("wrmix", [D])
        dt["wrffn"] = ein("wrffn", [D])
        for n in ("Wg", "Wv", "Wd"):
            dt[n] = ein(n, [D, D])
        for n in ("bg", "bv", "bd"):
            dt[n] = ein(n, [D])
        dt["Wgate"] = ein("Wgate", [D, FF])
        dt["Wup"] = ein("Wup", [D, FF])
        dt["Wout"] = ein("Wout", [FF, D])
        dt["fwg"] = ein("fwg", [II, D])
        dt["fwu"] = ein("fwu", [II, D])
        dt["fwd"] = ein("fwd", [D, II])
        dt["y"] = nc.dram_tensor("y", [S, D], F32, kind="ExternalOutput").ap()
        dt["s_sp"] = nc.dram_tensor("s_sp", [S, D], F32, kind="Internal").ap()
        dt["qx_sp"] = nc.dram_tensor("qx_sp", [D, S], BF16, kind="Internal").ap()
        dt["qh_sp"] = nc.dram_tensor("qh_sp", [TC, 32, P, P], BF16,
                                     kind="Internal").ap()
        self.taps = {}
        tapspec = {
            "xn": ([P, TC * D], F32), "xatt": ([P, TC * D], F32),
            "s": ([S, D], F32), "xnb": ([P, 8, S], BF16), "qx": ([D, S], BF16),
            "stats": ([P, 352], F32), "qwgT": ([D, II], BF16),
            "qwdT": ([II, D], BF16), "h4": ([P, II], F32), "g4": ([P, II], F32), "qh": ([TC, 32, P, P], BF16),
        }
        for name in self.debug:
            shp, dty = tapspec[name]
            self.taps[name] = nc.dram_tensor("tap_" + name, shp, dty,
                                             kind="ExternalOutput").ap()

        self.sq = Sem(es, nc, "sq")
        self.gq = Sem(es, nc, "gq")
        self.sT = Sem(es, nc, "sT")
        self.sV = Sem(es, nc, "sV")
        self.sS = Sem(es, nc, "sS")
        self.sG = Sem(es, nc, "sG")
        self.all_sems = [self.sq, self.gq, self.sT, self.sV, self.sS, self.sG]

        sb = lambda n, shp, dty=F32: es.enter_context(nc.sbuf_tensor(n, shp, dty))
        self.ident_b = sb("ident_b", [P, P], BF16)
        self.ones128 = sb("ones128", [P, P], F32)
        self.identf = sb("identf", [P, P], F32)
        self.wrmix_c = sb("wrmix_c", [P, 8])
        self.wrffn_c = sb("wrffn_c", [P, 8])
        self.bg_c = sb("bg_c", [P, 8])
        self.bv_c = sb("bv_c", [P, 8])
        self.bd_c = sb("bd_c", [P, 8])
        self.st = {}
        for n in ("s1", "s2", "mm", "var", "r_ln", "nb_ln", "s2n", "r_mix",
                  "amax", "sxinv", "sx", "s2f", "r_ffn", "kq", "amaxh",
                  "m2", "gsc", "t0", "t1", "t2", "clipd"):
            self.st[n] = sb("st_" + n, [P, 16])
        self.wq = {}
        for n in ("sabs", "sabsB", "mwg", "mwu", "mwd", "thrg", "nthrg", "thru",
                  "nthru", "thrd", "nthrd"):
            self.wq[n] = sb("wq_" + n, [P, 1])
        self.junkA = sb("junkA", [P, 64])
        self.p1acc = {n: sb("p1acc_" + n, [P, 32]) for n in ("fwg", "fwu", "fwd")}
        self.p1_state = {}
        self.ceps = sb("ceps", [P, 2])
        nc.const_aps.register(F32, 1e-5, self.ceps[:, 0:1])
        nc.const_aps.register(F32, 1e-6, self.ceps[:, 1:2])
        self.ARN = es.enter_context(
            nc.sbuf_tensor("ARN", [P, 192 * KB], mybir.dt.uint8))
        self.arena_base = nc.lookup_mloc(self.ARN).addr
        self._vid = 0

    def sg(self, sem=None, k=1):
        i = self.nc.scalar.activation(self.junkA[:, 0:64], self.junkA[:, 0:64],
                                      AF.Identity)
        if sem is not None:
            sem.inc(i, k)
        return i

    def vg(self, sem=None, k=1):
        i = self.nc.vector.tensor_scalar(self.junkA.ap(), self.junkA.ap(),
                                         1.0, None, ALU.mult)
        if sem is not None:
            sem.inc(i, k)
        return i


    def p1_gen(self, wn, slabs, junk, modes, width):
        nc = self.nc
        V, SC, SY = nc.vector, nc.scalar, nc.sync
        sq, sV, sS = self.sq, self.sV, self.sS
        acc = self.p1acc[wn]
        nsl = (II * D) // (128 * width)
        free = [None, None]
        for s2_ in range(nsl):
            sb_ = s2_ % 2
            mode = modes[s2_ % len(modes)]
            if free[sb_] is not None:
                SY.wait_ge(free[sb_][0].h, free[sb_][1])
            sq.inc(SY.dma_start(
                out=slabs[sb_].ap(),
                in_=self.dram[wn].rearrange("a d -> (a d)").rearrange(
                    "(s p w) -> s p w", p=P, w=width)[s2_]), 16)
            if mode == "V":
                V.wait_ge(sq.h, sq.n)
                V.reduce_sum(acc[:, s2_:s2_ + 1], slabs[sb_].ap(),
                             AX.X, apply_absolute_value=True)
                self.vg()
                self.vg(sV)
                free[sb_] = (sV, sV.n)
            else:
                SC.wait_ge(sq.h, sq.n)
                SC.activation(junk[:, 0:width], slabs[sb_].ap(), AF.Abs,
                              accum_out=acc[:, s2_:s2_ + 1])
                self.sg()
                self.sg(sS)
                free[sb_] = (sS, sS.n)
            yield
        self.p1_state[wn] = (free, nsl)

    def pump(self, gen):
        if gen is not None:
            next(gen, None)

    def drain(self, gen):
        if gen is not None:
            for _ in gen:
                pass

    def view(self, shape, dtype, off_kb):
        self._vid += 1
        return self.nc.alloc_sbuf_tensor_at(
            f"v{self._vid}", shape, dtype,
            offset=self.arena_base + int(round(off_kb * KB)))

    def phase_A(self):
        nc, st, dram = self.nc, self.st, self.dram
        sq, gq, sT, sV, sS, sG = self.sq, self.gq, self.sT, self.sV, self.sS, self.sG
        V, SC, T, SY, G = nc.vector, nc.scalar, nc.tensor, nc.sync, nc.gpsimd

        self.xn_tm = self.view([P, TC, D], F32, 0)
        self.x_tm = self.view([P, TC, D], F32, 64)
        self.xnb = self.view([P, 8, S], BF16, 128)
        xn2b_q = self.view([P, 4, D], BF16, 160)
        qx_q = self.view([P, 4, D], BF16, 168)
        sqscr = self.view([P, D], F32, 176)
        qtmp = self.view([P, D], F32, 180)
        qxstage = self.view([P, 8, 512], BF16, 184)

        sq.inc(SY.dma_start(out=self.x_tm.ap(),
                            in_=dram["x"].rearrange("(a p) d -> p a d", p=P)), 16)
        for nm, dst in (("wrmix", self.wrmix_c), ("wrffn", self.wrffn_c),
                        ("bg", self.bg_c), ("bv", self.bv_c), ("bd", self.bd_c)):
            sq.inc(SY.dma_start(out=dst.ap(),
                                in_=dram[nm].rearrange("(c p) -> p c", p=P),
                                allow_slow_non_contiguous=True), 16)
        x_loaded = sq.n

        G.memset(self.ident_b.ap(), 0.0)
        G.affine_select(out=self.ident_b.ap(), in_=self.ident_b.ap(),
                        compare_op=ALU.not_equal, fill=1.0, base=0,
                        pattern=[[-1, P]], channel_multiplier=1)
        G.memset(self.identf.ap(), 0.0)
        G.affine_select(out=self.identf.ap(), in_=self.identf.ap(),
                        compare_op=ALU.not_equal, fill=1.0, base=0,
                        pattern=[[-1, P]], channel_multiplier=1)
        sG.inc(G.memset(self.ones128.ap(), 1.0))
        self.idents_ready = sG.n

        SC.memzero(st["s2"].ap())
        SC.memzero(st["s2n"].ap())
        SC.memzero(st["s2f"].ap())
        SC.wait_ge(sq.h, x_loaded)
        for tc in range(TC):
            i = SC.activation(sqscr.ap(), self.x_tm[:, tc, :], AF.Square,
                              accum_out=st["s2"][:, tc:tc + 1])
        sS.inc(i)
        s2_done = sS.n

        V.wait_ge(sq.h, x_loaded)
        for tc in range(TC):
            V.reduce_sum(st["s1"][:, tc:tc + 1], self.x_tm[:, tc, :], AX.X)
        V.wait_ge(sS.h, s2_done)
        self.vg()
        self.vg()
        V.tensor_scalar(st["mm"].ap(), st["s1"].ap(), 1.0 / D, None, ALU.mult)
        self.vg()
        V.tensor_tensor(st["t0"].ap(), st["mm"].ap(), st["mm"].ap(), ALU.mult)
        self.vg()
        V.tensor_scalar(st["t0"].ap(), st["t0"].ap(), -1.0, None, ALU.mult)
        self.vg()
        V.scalar_tensor_tensor(st["var"].ap(), st["s2"].ap(), 1.0 / D,
                               st["t0"].ap(), ALU.mult, ALU.add)
        self.vg(sV)
        var_done = sV.n

        SC.wait_ge(sV.h, var_done)
        SC.activation(st["t1"].ap(), st["var"].ap(), AF.Ln, bias=1e-5, scale=1.0)
        self.sg()
        SC.activation(st["r_ln"].ap(), st["t1"].ap(), AF.Exp, scale=-0.5)
        self.sg(sS)
        rln_done = sS.n

        V.wait_ge(sS.h, rln_done)
        V.tensor_tensor(st["nb_ln"].ap(), st["mm"].ap(), st["r_ln"].ap(), ALU.mult)
        self.vg()
        V.tensor_scalar(st["nb_ln"].ap(), st["nb_ln"].ap(), -1.0, None, ALU.mult)
        self.vg(sV)
        nb_done = sV.n

        SC.wait_ge(sV.h, nb_done)
        for tc in range(TC):
            i = SC.activation(self.xn_tm[:, tc, :], self.x_tm[:, tc, :],
                              AF.Identity, bias=st["nb_ln"][:, tc:tc + 1],
                              scale=st["r_ln"][:, tc:tc + 1])
        sS.inc(i)
        xn_done = sS.n
        for tc in range(TC):
            SC.activation(sqscr.ap(), self.xn_tm[:, tc, :], AF.Square,
                          accum_out=st["s2n"][:, tc:tc + 1])
        SC.activation(st["t1"].ap(), st["s2n"].ap(), AF.Ln, bias=1e-6, scale=1.0 / D)
        self.sg()
        SC.activation(st["r_mix"].ap(), st["t1"].ap(), AF.Exp, scale=-0.5)
        self.sg(sS)
        rmix_done = sS.n

        V.wait_ge(sS.h, xn_done)
        for tc in range(TC):
            V.reduce_max(st["amax"][:, tc:tc + 1], self.xn_tm[:, tc, :], AX.X,
                         apply_absolute_value=True)
        self.vg()
        V.tensor_scalar(st["sxinv"].ap(), st["amax"].ap(), 1e-5,
                        1.0 / 127.0, ALU.max, ALU.mult)
        self.vg(sV)
        sxinv_done = sV.n
        SC.wait_ge(sV.h, sxinv_done)
        SC.activation(st["t1"].ap(), st["sxinv"].ap(), AF.Ln)
        self.sg()
        SC.activation(st["sx"].ap(), st["t1"].ap(), AF.Exp, scale=-1.0)
        self.sg(sS)
        sx_done = sS.n

        with contextlib.ExitStack() as pes:
            pb = [pes.enter_context(nc.psum_tensor(f"pA{i}", [P, 512], BF16))
                  for i in range(4)]
            bank_free = [None, None, None, None]
            q_trans_done = None
            for q in range(4):
                V.wait_ge(sS.h, sx_done)
                if q_trans_done is not None:
                    V.wait_ge(sT.h, q_trans_done)
                for k in range(4):
                    tc = q * 4 + k
                    V.tensor_scalar(qtmp.ap(), self.xn_tm[:, tc, :],
                                    st["sx"][:, tc:tc + 1], MAGIC, ALU.mult, ALU.add)
                    i = V.tensor_scalar(qx_q[:, k, :], qtmp.ap(), -MAGIC,
                                        None, ALU.add)
                sV.inc(i)
                qxq_done = sV.n
                SC.wait_ge(sS.h, rmix_done)
                if q_trans_done is not None:
                    SC.wait_ge(sT.h, q_trans_done)
                for k in range(4):
                    tc = q * 4 + k
                    i = SC.activation(xn2b_q[:, k, :], self.xn_tm[:, tc, :],
                                      AF.Identity, scale=st["r_mix"][:, tc:tc + 1])
                sS.inc(i)
                x2b_done = sS.n

                T.wait_ge(sV.h, qxq_done)
                T.wait_ge(sS.h, x2b_done)
                T.wait_ge(sG.h, self.idents_ready)
                for fc in range(8):
                    b = fc % 2
                    if bank_free[b] is not None:
                        T.wait_ge(bank_free[b][0].h, bank_free[b][1])
                    for k in range(4):
                        i = T.transpose(pb[b][:, k * 128:(k + 1) * 128],
                                        xn2b_q[:, k, fc * 128:(fc + 1) * 128],
                                        self.ident_b.ap())
                    sT.inc(i)
                    tcnt = sT.n
                    SC.wait_ge(sT.h, tcnt)
                    sS.inc(SC.activation(self.xnb[:, fc, q * 512:(q + 1) * 512],
                                         pb[b].ap(), AF.Identity,
                                         scale=self.wrmix_c[:, fc:fc + 1]))
                    bank_free[b] = (sS, sS.n)
                for jc in range(8):
                    b = 2 + jc % 2
                    if bank_free[b] is not None:
                        T.wait_ge(bank_free[b][0].h, bank_free[b][1])
                    for k in range(4):
                        i = T.transpose(pb[b][:, k * 128:(k + 1) * 128],
                                        qx_q[:, k, jc * 128:(jc + 1) * 128],
                                        self.ident_b.ap())
                    sT.inc(i)
                    tcnt = sT.n
                    V.wait_ge(sT.h, tcnt)
                    sV.inc(V.tensor_copy(qxstage[:, jc, :], pb[b].ap()))
                    bank_free[b] = (sV, sV.n)
                stage_done = sV.n
                q_trans_done = sT.n
                SY.wait_ge(sV.h, stage_done)
                sq.inc(SY.dma_start(
                    out=dram["qx_sp"].rearrange("(a p) t -> p a t", p=P)[:, :, q * 512:(q + 1) * 512],
                    in_=qxstage.ap()), 16)
        self.qx_spilled = sq.n

        if self.taps:
            SY.wait_ge(sS.h, sS.n)
            SY.wait_ge(sV.h, sV.n)
            if "xn" in self.taps:
                sq.inc(SY.dma_start(out=self.taps["xn"],
                                    in_=self.xn_tm.ap().rearrange("p a d -> p (a d)")), 16)
            if "stats" in self.taps:
                for i2, nm in enumerate(("mm", "var", "r_ln", "r_mix", "amax",
                                         "sxinv", "sx", "s2", "s2n", "s1")):
                    sq.inc(SY.dma_start(out=self.taps["stats"][:, i2 * 16:(i2 + 1) * 16],
                                        in_=st[nm].ap()), 16)
            if "xnb" in self.taps:
                sq.inc(SY.dma_start(out=self.taps["xnb"], in_=self.xnb.ap()), 16)
            if "qx" in self.taps:
                SY.wait_ge(sq.h, self.qx_spilled)
                sq.inc(SY.dma_start(out=self.taps["qx"], in_=dram["qx_sp"]), 16)

    def phase_C1(self):
        nc, st, dram = self.nc, self.st, self.dram
        sq, gq, sT, sV, sS, sG = self.sq, self.gq, self.sT, self.sV, self.sS, self.sG
        V, SC, T, SY, G = nc.vector, nc.scalar, nc.tensor, nc.sync, nc.gpsimd

        self.xatt = self.view([P, TC, D], F32, 64)
        wtile = [self.view([P, 3, 8, 128], BF16, 160),
                 self.view([P, 3, 8, 128], BF16, 166)]
        xs = self.view([P, S], F32, 172)
        ah = self.view([P, S], F32, 180)
        t1 = self.view([P, 512], F32, 188)
        t2 = self.view([P, 512], F32, 190)

        with contextlib.ExitStack() as pes:
            pgvd = [pes.enter_context(nc.psum_tensor(f"pC{i}", [P, 512], F32))
                    for i in range(6)]
            ph = [pes.enter_context(nc.psum_tensor(f"pH{i}", [P, 512], F32))
                  for i in range(2)]
            xnb_done = sS.n
            gvd_free = [None, None]
            ph_free = [None, None]
            wfree = [None, None]
            h_consumed_t = None
            ep_v_last = None

            for jc in range(8):
                wb = jc % 2
                if wfree[wb] is not None:
                    G.wait_ge(sT.h, wfree[wb])
                for wi, wn in enumerate(("Wg", "Wv", "Wd")):
                    gq.inc(G.dma_start(
                        out=wtile[wb][:, wi, :, :],
                        in_=dram[wn][:, jc * 128:(jc + 1) * 128]
                            .rearrange("(a p) j -> p a j", p=P)), 16)
                w_ready = gq.n

                T.wait_ge(gq.h, w_ready)
                for sl in range(4):
                    pset = sl % 2
                    if gvd_free[pset] is not None:
                        T.wait_ge(gvd_free[pset][0].h, gvd_free[pset][1])
                    for wi in range(3):
                        pp = pgvd[pset * 3 + wi]
                        for ic in range(8):
                            i = T.matmul(pp.ap(), wtile[wb][:, wi, ic, :],
                                         self.xnb[:, ic, sl * 512:(sl + 1) * 512],
                                         start=(ic == 0), stop=(ic == 7))
                    sT.inc(i)
                    mm_done = sT.n

                    SC.wait_ge(sT.h, mm_done)
                    if sl == 0 and h_consumed_t is not None:
                        SC.wait_ge(sT.h, h_consumed_t)
                    if ep_v_last is not None:
                        SC.wait_ge(sV.h, ep_v_last)
                    SC.activation(t1.ap(), pgvd[pset * 3 + 0].ap(), AF.Sigmoid,
                                  bias=self.bg_c[:, jc:jc + 1])
                    SC.activation(t2.ap(), pgvd[pset * 3 + 1].ap(), AF.Tanh,
                                  bias=self.bv_c[:, jc:jc + 1])
                    sS.inc(SC.activation(ah[:, sl * 512:(sl + 1) * 512],
                                         pgvd[pset * 3 + 2].ap(), AF.Sigmoid,
                                         bias=self.bd_c[:, jc:jc + 1]))
                    ep_s = sS.n
                    V.wait_ge(sS.h, ep_s)
                    V.tensor_tensor(xs[:, sl * 512:(sl + 1) * 512], t1.ap(),
                                    t2.ap(), ALU.mult)
                    sV.inc(V.tensor_scalar(ah[:, sl * 512:(sl + 1) * 512],
                                           ah[:, sl * 512:(sl + 1) * 512],
                                           0.998, 0.001, ALU.mult, ALU.add))
                    gvd_free[pset] = (sV, sV.n)
                    ep_v_last = sV.n
                sV.inc(V.tensor_tensor_scan(ah.ap(), ah.ap(), xs.ap(), 0.0,
                                            ALU.mult, ALU.add))
                scan_done = sV.n
                T.wait_ge(sV.h, scan_done)
                for tq in range(4):
                    b = tq % 2
                    if ph_free[b] is not None:
                        T.wait_ge(ph_free[b][0].h, ph_free[b][1])
                    for k in range(4):
                        tc = tq * 4 + k
                        i = T.transpose(ph[b][:, k * 128:(k + 1) * 128],
                                        ah[:, tc * 128:(tc + 1) * 128],
                                        self.identf.ap())
                    sT.inc(i)
                    tcnt = sT.n
                    V.wait_ge(sT.h, tcnt)
                    sV.inc(V.tensor_tensor(
                        self.xatt[:, tq * 4:tq * 4 + 4, jc * 128:(jc + 1) * 128],
                        self.xn_tm[:, tq * 4:tq * 4 + 4, jc * 128:(jc + 1) * 128],
                        ph[b].ap().rearrange("p (a t) -> p a t", a=4), ALU.add))
                    ph_free[b] = (sV, sV.n)
                h_consumed_t = sT.n
                wfree[wb] = mm_done

        if "xatt" in self.taps:
            SY.wait_ge(sV.h, sV.n)
            sq.inc(SY.dma_start(out=self.taps["xatt"],
                                in_=self.xatt.ap().rearrange("p a d -> p (a d)")), 16)

    def phase_C2pre(self):
        nc, st, dram = self.nc, self.st, self.dram
        sq, sT, sV, sS = self.sq, self.sT, self.sV, self.sS
        V, SC, T, SY = nc.vector, nc.scalar, nc.tensor, nc.sync
        self.ynb = self.view([P, 8, S], BF16, 128)
        x2b_q = self.view([P, 4, D], BF16, 168)
        sqscr = self.view([P, D], F32, 176)
        self.c2_wgt = [self.view([P, 8, 128], BF16, 160), self.view([P, 8, 128], BF16, 162)]
        self.c2_wut = [self.view([P, 8, 128], BF16, 164), self.view([P, 8, 128], BF16, 166)]
        self.c2_w_ready = [None, None]
        gq = self.gq
        G = nc.gpsimd
        for c in range(2):
            gq.inc(G.dma_start(out=self.c2_wgt[c].ap(),
                               in_=dram["Wgate"][:, c * 128:(c + 1) * 128]
                                   .rearrange("(a p) f -> p a f", p=P)), 16)
            gq.inc(G.dma_start(out=self.c2_wut[c].ap(),
                               in_=dram["Wup"][:, c * 128:(c + 1) * 128]
                                   .rearrange("(a p) f -> p a f", p=P)), 16)
            self.c2_w_ready[c] = gq.n

        for tc in range(TC):
            i = SC.activation(sqscr.ap(), self.xatt[:, tc, :], AF.Square,
                              accum_out=st["s2f"][:, tc:tc + 1])
        SC.activation(st["t1"].ap(), st["s2f"].ap(), AF.Ln, bias=1e-6, scale=1.0 / D)
        self.sg()
        SC.activation(st["r_ffn"].ap(), st["t1"].ap(), AF.Exp, scale=-0.5)
        self.sg(sS)
        rffn_done = sS.n

        with contextlib.ExitStack() as pes:
            pb = [pes.enter_context(nc.psum_tensor(f"pY{i}", [P, 512], BF16))
                  for i in range(2)]
            bank_free = [None, None]
            q_trans_done = None
            for q in range(4):
                SC.wait_ge(sS.h, rffn_done)
                if q_trans_done is not None:
                    SC.wait_ge(sT.h, q_trans_done)
                for k in range(4):
                    tc = q * 4 + k
                    i = SC.activation(x2b_q[:, k, :], self.xatt[:, tc, :],
                                      AF.Identity, scale=st["r_ffn"][:, tc:tc + 1])
                sS.inc(i)
                x2b_done = sS.n
                T.wait_ge(sS.h, x2b_done)
                for fc in range(8):
                    b = fc % 2
                    if bank_free[b] is not None:
                        T.wait_ge(bank_free[b][0].h, bank_free[b][1])
                    for k in range(4):
                        i = T.transpose(pb[b][:, k * 128:(k + 1) * 128],
                                        x2b_q[:, k, fc * 128:(fc + 1) * 128],
                                        self.ident_b.ap())
                    sT.inc(i)
                    SC.wait_ge(sT.h, sT.n)
                    sS.inc(SC.activation(self.ynb[:, fc, q * 512:(q + 1) * 512],
                                         pb[b].ap(), AF.Identity,
                                         scale=self.wrffn_c[:, fc:fc + 1]))
                    bank_free[b] = (sS, sS.n)
                q_trans_done = sT.n

    def phase_C2(self):
        nc, st, dram = self.nc, self.st, self.dram
        sq, gq, sT, sV, sS = self.sq, self.gq, self.sT, self.sV, self.sS
        V, SC, T, SY, G = nc.vector, nc.scalar, nc.tensor, nc.sync, nc.gpsimd
        p1slabs = [self.view([P, 1024], F32, 172), self.view([P, 1024], F32, 176)]
        p1junk = self.view([P, 1024], F32, 180)
        bg_fwg = self.p1_gen("fwg", p1slabs, p1junk, ["S"], 1024)

        H1 = self.view([P, 8, S], BF16, 0)
        woutT = [self.view([P, 8, D], BF16, 32), self.view([P, 8, D], BF16, 48)]
        wgt = self.c2_wgt
        wut = self.c2_wut
        t1 = [self.view([P, 512], F32, 168), self.view([P, 512], F32, 170)]

        with contextlib.ExitStack() as pes:
            pG = [pes.enter_context(nc.psum_tensor(f"pG{i}", [P, 512], F32))
                  for i in range(2)]
            pU = [pes.enter_context(nc.psum_tensor(f"pU{i}", [P, 512], F32))
                  for i in range(2)]
            pO = [pes.enter_context(nc.psum_tensor(f"pO{i}", [P, 1024], F32))
                  for i in range(2)]
            pGU_free = [None, None]
            pO_free = [None, None]
            w_free = [None, None]
            wo_free = [None, None]
            h1_ready_after_third = None

            for cb in range(3):
                wob = cb % 2
                if wo_free[wob] is not None:
                    G.wait_ge(sT.h, wo_free[wob])
                gq.inc(G.dma_start(
                    out=woutT[wob].ap(),
                    in_=dram["Wout"][cb * 1024:(cb + 1) * 1024, :]
                        .rearrange("(a p) j -> p a j", p=P)), 16)
                wout_ready = gq.n
                for c8 in range(8):
                    c = cb * 8 + c8
                    wb = c8 % 2
                    if c < 2:
                        w_ready = self.c2_w_ready[c]
                    else:
                        if w_free[wb] is not None:
                            G.wait_ge(sT.h, w_free[wb])
                        gq.inc(G.dma_start(
                            out=wgt[wb].ap(),
                            in_=dram["Wgate"][:, c * 128:(c + 1) * 128]
                                .rearrange("(a p) f -> p a f", p=P)), 16)
                        gq.inc(G.dma_start(
                            out=wut[wb].ap(),
                            in_=dram["Wup"][:, c * 128:(c + 1) * 128]
                                .rearrange("(a p) f -> p a f", p=P)), 16)
                        w_ready = gq.n
                    T.wait_ge(gq.h, w_ready)
                    for sl in range(4):
                        ps = sl % 2
                        if pGU_free[ps] is not None:
                            T.wait_ge(pGU_free[ps][0].h, pGU_free[ps][1])
                        for ic in range(8):
                            T.matmul(pG[ps].ap(), wgt[wb][:, ic, :],
                                     self.ynb[:, ic, sl * 512:(sl + 1) * 512],
                                     start=(ic == 0), stop=(ic == 7))
                        for ic in range(8):
                            i = T.matmul(pU[ps].ap(), wut[wb][:, ic, :],
                                         self.ynb[:, ic, sl * 512:(sl + 1) * 512],
                                         start=(ic == 0), stop=(ic == 7))
                        sT.inc(i)
                        mm_done = sT.n
                        SC.wait_ge(sT.h, mm_done)
                        sS.inc(SC.activation(t1[ps].ap(), pG[ps].ap(), AF.Silu))
                        sil_done = sS.n
                        V.wait_ge(sS.h, sil_done)
                        sV.inc(V.tensor_tensor(H1[:, c8, sl * 512:(sl + 1) * 512],
                                               t1[ps].ap(), pU[ps].ap(), ALU.mult))
                        pGU_free[ps] = (sV, sV.n)
                    w_free[wb] = sT.n
                    self.pump(bg_fwg)
                h1_done = sV.n
                T.wait_ge(sV.h, h1_done)
                T.wait_ge(gq.h, wout_ready)
                for tc in range(TC):
                    po = tc % 2
                    if pO_free[po] is not None:
                        T.wait_ge(pO_free[po][0].h, pO_free[po][1])
                    for c8 in range(8):
                        for half in range(2):
                            i = T.matmul(pO[po][:, half * 512:(half + 1) * 512],
                                         H1[:, c8, tc * 128:(tc + 1) * 128],
                                         woutT[wob][:, c8, half * 512:(half + 1) * 512],
                                         start=(c8 == 0), stop=(c8 == 7))
                    sT.inc(i)
                    ob_done = sT.n
                    V.wait_ge(sT.h, ob_done)
                    sV.inc(V.tensor_tensor(self.xatt[:, tc, :],
                                           self.xatt[:, tc, :], pO[po].ap(),
                                           ALU.add))
                    pO_free[po] = (sV, sV.n)
                    self.pump(bg_fwg)
                wo_free[wob] = sT.n

        self.drain(bg_fwg)
        SY.wait_ge(sV.h, sV.n)
        sq.inc(SY.dma_start(out=dram["s_sp"].rearrange("(a p) d -> p a d", p=P),
                            in_=self.xatt.ap()), 16)
        self.s_spilled = sq.n
        if "s" in self.taps:
            sq.inc(SY.dma_start(out=self.taps["s"].rearrange("(a p) d -> p a d", p=P),
                                in_=self.xatt.ap()), 16)

    def phase_D0(self):
        nc, st, dram = self.nc, self.st, self.dram
        sq, gq, sT, sV, sS, sG = self.sq, self.gq, self.sT, self.sV, self.sS, self.sG
        V, SC, T, SY, G = nc.vector, nc.scalar, nc.tensor, nc.sync, nc.gpsimd
        wq = self.wq

        psum_scope = contextlib.ExitStack()
        pst = psum_scope.enter_context(nc.psum_tensor("pst", [P, 512], F32))

        def thresholds(wn, mw, thr, nthr):
            f, nsl = self.p1_state[wn]
            T.wait_ge(f[0][0].h, f[0][1])
            T.wait_ge(f[1][0].h, f[1][1])
            T.wait_ge(sG.h, self.idents_ready)
            T.wait_ge(sV.h, sV.n)
            sT.inc(T.matmul(pst[:, 0:nsl], self.ones128.ap(),
                            self.p1acc[wn][:, 0:nsl], start=True, stop=True))
            V.wait_ge(sT.h, sT.n)
            V.reduce_sum(wq[mw].ap(), pst[:, 0:nsl], AX.X)
            self.vg()
            V.tensor_scalar(wq[mw].ap(), wq[mw].ap(), 1.0 / (II * D), 1e-5,
                            ALU.mult, ALU.max)
            self.vg()
            V.tensor_scalar(wq[thr].ap(), wq[mw].ap(), 0.5, None, ALU.mult)
            self.vg()
            V.tensor_scalar(wq[nthr].ap(), wq[thr].ap(), -1.0, None, ALU.mult)
            self.vg(sV)

        thresholds("fwg", "mwg", "thrg", "nthrg")
        fwu_slabs = [self.view([P, 8192], F32, 64), self.view([P, 8192], F32, 96)]
        fwu_junk = self.view([P, 8192], F32, 128)
        bg_fwu = self.p1_gen("fwu", fwu_slabs, fwu_junk, ["S"], 8192)
        fwd_slabs = [self.view([P, 4096], F32, 128), self.view([P, 4096], F32, 144)]
        bg_fwd = self.p1_gen("fwd", fwd_slabs, None, ["V"], 4096)
        self.qwgT = self.view([P, 8, II], BF16, 0)
        self.qwuT = self.view([P, 8, II], BF16, 64)
        wsl = [self.view([P, 2, D], F32, 160), self.view([P, 2, D], F32, 168)]
        ctmp = self.view([P, 2 * D], F32, 176)
        qsl = self.view([P, 2, D], BF16, 184)
        with contextlib.ExitStack() as pes:
            pb = [pes.enter_context(nc.psum_tensor(f"pW{i}", [P, 512], BF16))
                  for i in range(2)]
            slab_free = [None, None]
            bank_free = [None, None]
            last_trans = None
            for mi, (wn, dst, thr, nthr) in enumerate((
                    ("fwg", self.qwgT, "thrg", "nthrg"),
                    ("fwu", self.qwuT, "thru", "nthru"))):
                for sl in range(16):
                    sb_ = sl % 2
                    if slab_free[sb_] is not None:
                        SY.wait_ge(sV.h, slab_free[sb_])
                    sq.inc(SY.dma_start(
                        out=wsl[sb_].ap(),
                        in_=dram[wn].rearrange("(s a p) j -> s p a j", a=2, p=P)[sl]), 16)
                    dma_n = sq.n
                    V.wait_ge(sq.h, dma_n)
                    flat = wsl[sb_].ap().rearrange("p a j -> p (a j)")
                    V.tensor_scalar(ctmp.ap(), flat, wq[thr].ap(), None, ALU.is_gt)
                    V.tensor_scalar(flat, flat, wq[nthr].ap(), None, ALU.is_lt)
                    if last_trans is not None:
                        V.wait_ge(sT.h, last_trans)
                    sV.inc(V.tensor_tensor(qsl.ap().rearrange("p a j -> p (a j)"),
                                           ctmp.ap(), flat, ALU.subtract))
                    q_done = sV.n
                    slab_free[sb_] = q_done
                    slab_free[sb_] = q_done
                    T.wait_ge(sV.h, q_done)
                    for jc in range(8):
                        b = jc % 2
                        if bank_free[b] is not None:
                            T.wait_ge(bank_free[b][0].h, bank_free[b][1])
                        for a in range(2):
                            i = T.transpose(pb[b][:, a * 128:(a + 1) * 128],
                                            qsl[:, a, jc * 128:(jc + 1) * 128],
                                            self.ident_b.ap())
                        sT.inc(i)
                        SC.wait_ge(sT.h, sT.n)
                        sS.inc(SC.copy(dst[:, jc, sl * 256:(sl + 1) * 256],
                                       pb[b][:, 0:256]))
                        bank_free[b] = (sS, sS.n)
                    last_trans = sT.n
                    self.pump(bg_fwu if mi == 0 else bg_fwd)
                if mi == 0:
                    self.drain(bg_fwu)
                    thresholds("fwu", "mwu", "thru", "nthru")
            self.drain(bg_fwd)
            thresholds("fwd", "mwd", "thrd", "nthrd")
        psum_scope.close()

        V.tensor_tensor(st["t0"].ap(), st["sxinv"].ap(), st["sxinv"].ap(), ALU.mult)
        self.vg()
        V.tensor_tensor(st["kq"].ap(), st["t0"].ap(), st["sxinv"].ap(), ALU.mult)
        self.vg()
        V.tensor_scalar(st["kq"].ap(), st["kq"].ap(), wq["mwg"].ap(), None, ALU.mult)
        self.vg()
        V.tensor_scalar(st["kq"].ap(), st["kq"].ap(), wq["mwg"].ap(), None, ALU.mult)
        self.vg()
        V.tensor_scalar(st["kq"].ap(), st["kq"].ap(), wq["mwu"].ap(), None, ALU.mult)
        self.vg(sV)
        if "qwgT" in self.taps:
            SY.wait_ge(sV.h, sV.n)
            SY.wait_ge(sS.h, sS.n)
            sq.inc(SY.dma_start(out=self.taps["qwgT"].rearrange("(a p) i -> p a i", p=P),
                                in_=self.qwgT.ap()), 16)
        if "stats" in self.taps:
            SY.wait_ge(sV.h, sV.n)
            SY.wait_ge(sS.h, sS.n)
            sq.inc(SY.dma_start(out=self.taps["stats"][:, 16:24], in_=acc["fwg"].ap()), 16)
            sq.inc(SY.dma_start(out=self.taps["stats"][:, 24:32], in_=acc["fwd"].ap()), 16)
            for i2, nm in enumerate(("mwg", "thrg", "mwu", "thru", "mwd", "thrd", "sabs")):
                sq.inc(SY.dma_start(out=self.taps["stats"][:, i2:i2 + 1],
                                    in_=wq[nm].ap(),
                                    allow_slow_non_contiguous=True), 16)

    SERIAL_D1 = False

    def phase_D1(self):
        nc, st, dram = self.nc, self.st, self.dram
        sq, sT, sV, sS = self.sq, self.sT, self.sV, self.sS
        V, SC, T, SY = nc.vector, nc.scalar, nc.tensor, nc.sync
        wq = self.wq

        qx_t = [self.view([P, 8, 128], BF16, 160), self.view([P, 8, 128], BF16, 162)]
        qhT = self.view([P, 32, 128], BF16, 176)
        qh_tm = self.view([P, II], BF16, 184)
        g_tm = self.view([P, II], F32, 128)
        h_tm = self.view([P, II], F32, 144)
        qx_used = [None, None]
        qx_ready = [None, None]

        with contextlib.ExitStack() as pes:
            pQ = [pes.enter_context(nc.psum_tensor(f"pQ{i}", [P, 1024], F32))
                  for i in range(3)]
            pqh = pes.enter_context(nc.psum_tensor("pqh", [P, 1024], BF16))
            pQ_free = [None, None, None]
            pqh_free = None
            stage_free = None
            trans_done = None
            pending = None
            prev_qh_done = None
            rot = [0]
            for tc in range(TC):
                qb = tc % 2
                if qx_used[qb] is not None:
                    SY.wait_ge(sT.h, qx_used[qb])
                sq.inc(SY.dma_start(
                    out=qx_t[qb].ap(),
                    in_=dram["qx_sp"].rearrange("(a p) t -> p a t", p=P)[:, :, tc * 128:(tc + 1) * 128]), 16)
                qx_ready[qb] = sq.n
                T.wait_ge(sq.h, qx_ready[qb])
                g_evac_done = [None] * 4
                sq_done_q = [None] * 4
                if prev_qh_done is not None:
                    SC.wait_ge(sV.h, prev_qh_done)
                for qq in range(4):
                    ps = qq % 3
                    if pQ_free[ps] is not None:
                        T.wait_ge(pQ_free[ps][0].h, pQ_free[ps][1])
                    for j in range(8):
                        for nb in range(2):
                            i = T.matmul(pQ[ps][:, nb * 512:(nb + 1) * 512],
                                         qx_t[qb][:, j, :],
                                         self.qwgT[:, j, qq * 1024 + nb * 512:
                                                   qq * 1024 + (nb + 1) * 512],
                                         start=(j == 0), stop=(j == 7))
                    sT.inc(i)
                    SC.wait_ge(sT.h, sT.n)
                    sS.inc(SC.activation(g_tm[:, qq * 1024:(qq + 1) * 1024],
                                         pQ[ps].ap(), AF.Relu))
                    g_evac_done[qq] = sS.n
                    pQ_free[ps] = (sS, sS.n)
                g_done = sV.n
                if "g4" in self.taps and tc == 1:
                    SY.wait_ge(sV.h, g_done)
                    sq.inc(SY.dma_start(out=self.taps["g4"], in_=g_tm.ap()), 16)
                for qq in range(4):
                    sS.inc(SC.activation(h_tm[:, qq * 1024:(qq + 1) * 1024],
                                         g_tm[:, qq * 1024:(qq + 1) * 1024],
                                         AF.Square))
                    sq_done_q[qq] = sS.n
                for qq in range(4):
                    ps = qq % 3
                    if pQ_free[ps] is not None:
                        T.wait_ge(pQ_free[ps][0].h, pQ_free[ps][1])
                    for j in range(8):
                        for nb in range(2):
                            i = T.matmul(pQ[ps][:, nb * 512:(nb + 1) * 512],
                                         qx_t[qb][:, j, :],
                                         self.qwuT[:, j, qq * 1024 + nb * 512:
                                                   qq * 1024 + (nb + 1) * 512],
                                         start=(j == 0), stop=(j == 7))
                    sT.inc(i)
                    V.wait_ge(sT.h, sT.n)
                    V.wait_ge(sS.h, sq_done_q[qq])
                    sV.inc(V.tensor_tensor(h_tm[:, qq * 1024:(qq + 1) * 1024],
                                           h_tm[:, qq * 1024:(qq + 1) * 1024],
                                           pQ[ps].ap(), ALU.mult))
                    pQ_free[ps] = (sV, sV.n)
                qx_used[qb] = sT.n
                if pending is not None:
                    emit_transposes(*pending)
                    pending = None
                V.reduce_max(st["amaxh"][:, tc:tc + 1], h_tm.ap(), AX.X,
                             apply_absolute_value=True)
                self.vg()
                V.tensor_tensor(st["t0"][:, tc:tc + 1], st["kq"][:, tc:tc + 1],
                                st["amaxh"][:, tc:tc + 1], ALU.mult)
                self.vg()
                V.tensor_scalar(st["clipd"][:, tc:tc + 1],
                                st["t0"][:, tc:tc + 1], 1e-5, None, ALU.max)
                self.vg(sV)
                clip_done = sV.n
                SC.wait_ge(sV.h, clip_done)
                SC.activation(st["t1"][:, tc:tc + 1], st["clipd"][:, tc:tc + 1], AF.Ln)
                self.sg()
                SC.activation(st["t2"][:, tc:tc + 1], st["t1"][:, tc:tc + 1],
                              AF.Exp, scale=-1.0)
                self.sg(sS)
                sh_done = sS.n
                V.wait_ge(sS.h, sh_done)
                V.tensor_scalar(st["t2"][:, tc:tc + 1], st["t2"][:, tc:tc + 1],
                                127.0, None, ALU.mult)
                self.vg()
                V.tensor_tensor(st["m2"][:, tc:tc + 1], st["kq"][:, tc:tc + 1],
                                st["t2"][:, tc:tc + 1], ALU.mult)
                self.vg()
                V.tensor_scalar(st["gsc"][:, tc:tc + 1], st["clipd"][:, tc:tc + 1],
                                1.0 / 127.0, None, ALU.mult)
                self.vg()
                V.tensor_scalar(st["gsc"][:, tc:tc + 1],
                                st["gsc"][:, tc:tc + 1], wq["mwd"].ap(),
                                None, ALU.mult)
                self.vg(sV)
                if trans_done is not None:
                    V.wait_ge(sT.h, trans_done)
                V.tensor_scalar(g_tm.ap(), h_tm.ap(), st["m2"][:, tc:tc + 1],
                                MAGIC, ALU.mult, ALU.add)
                sV.inc(V.tensor_scalar(qh_tm.ap(), g_tm.ap(), -MAGIC, None, ALU.add))
                qh_done = sV.n
                prev_qh_done = qh_done
                pend = (tc, qh_done)
                if "h4" in self.taps and tc == 1:
                    SY.wait_ge(sV.h, qh_done)
                    sq.inc(SY.dma_start(out=self.taps["h4"], in_=h_tm.ap()), 16)

                def emit_transposes(ptc, pqh_done):
                    nonlocal pqh_free, stage_free, trans_done
                    T.wait_ge(sV.h, pqh_done)
                    if stage_free is not None:
                        T.wait_ge(sq.h, stage_free)
                    for grp in range(4):
                        if pqh_free is not None:
                            T.wait_ge(pqh_free[0].h, pqh_free[1])
                        for k in range(8):
                            ib = grp * 8 + k
                            i = T.transpose(pqh[:, k * 128:(k + 1) * 128],
                                            qh_tm[:, ib * 128:(ib + 1) * 128],
                                            self.ident_b.ap())
                        sT.inc(i)
                        V.wait_ge(sT.h, sT.n)
                        sV.inc(V.tensor_copy(
                            qhT[:, grp * 8:(grp + 1) * 8, :],
                            pqh.ap().rearrange("p (k t) -> p k t", k=8)))
                        pqh_free = (sV, sV.n)
                    trans_done = sT.n
                    SY.wait_ge(sV.h, sV.n)
                    sq.inc(SY.dma_start(
                        out=dram["qh_sp"][ptc].rearrange("a p t -> p a t"),
                        in_=qhT.ap()), 16)
                    stage_free = sq.n
                if tc == TC - 1:
                    emit_transposes(*pend)
                    pending = None
                else:
                    pending = pend
        if "qh" in self.taps:
            SY.wait_ge(sq.h, sq.n)
            sq.inc(SY.dma_start(out=self.taps["qh"], in_=dram["qh_sp"]), 16)
        if "stats" in self.taps:
            SY.wait_ge(sV.h, sV.n)
            SY.wait_ge(sS.h, sS.n)
            for i2, nm in enumerate(("amaxh", "kq", "m2", "gsc", "clipd", "t2")):
                sq.inc(SY.dma_start(
                    out=self.taps["stats"][:, 160 + i2 * 16:160 + (i2 + 1) * 16],
                    in_=st[nm].ap()), 16)

    def phase_D2(self):
        nc, st, dram = self.nc, self.st, self.dram
        sq, sT, sV, sS = self.sq, self.sT, self.sV, self.sS
        V, SC, T, SY = nc.vector, nc.scalar, nc.tensor, nc.sync
        wq = self.wq

        self.qwdT = self.view([P, 32, D], BF16, 0)
        wsl = [self.view([P, 2, II], F32, 64), self.view([P, 2, II], F32, 96)]
        ctmp = self.view([P, 2 * II], F32, 160)
        qsl2 = [self.view([P, 2, II], BF16, 128), self.view([P, 2, II], BF16, 144)]
        with contextlib.ExitStack() as pes:
            pb = [pes.enter_context(nc.psum_tensor(f"pWD{i}", [P, 512], BF16))
                  for i in range(2)]
            slab_free = [None, None]
            bank_free = [None, None]
            qsl_trans = [None, None]
            for sl in range(4):
                sb_ = sl % 2
                if slab_free[sb_] is not None:
                    SY.wait_ge(sV.h, slab_free[sb_])
                sq.inc(SY.dma_start(
                    out=wsl[sb_].ap(),
                    in_=dram["fwd"].rearrange("(s a p) i -> s p a i", a=2, p=P)[sl]), 16)
                V.wait_ge(sq.h, sq.n)
                flat = wsl[sb_].ap().rearrange("p a i -> p (a i)")
                V.tensor_scalar(ctmp.ap(), flat, wq["thrd"].ap(), None, ALU.is_gt)
                V.tensor_scalar(flat, flat, wq["nthrd"].ap(), None, ALU.is_lt)
                if qsl_trans[sb_] is not None:
                    V.wait_ge(sT.h, qsl_trans[sb_])
                sV.inc(V.tensor_tensor(qsl2[sb_].ap().rearrange("p a i -> p (a i)"),
                                       ctmp.ap(), flat, ALU.subtract))
                q_done = sV.n
                slab_free[sb_] = q_done
                T.wait_ge(sV.h, q_done)
                for ib in range(32):
                    b = ib % 2
                    if bank_free[b] is not None:
                        T.wait_ge(bank_free[b][0].h, bank_free[b][1])
                    for a in range(2):
                        i = T.transpose(pb[b][:, a * 128:(a + 1) * 128],
                                        qsl2[sb_][:, a, ib * 128:(ib + 1) * 128],
                                        self.ident_b.ap())
                    sT.inc(i)
                    eng = (V, sV) if ib % 2 == 0 else (SC, sS)
                    E, sE = eng
                    E.wait_ge(sT.h, sT.n)
                    if E is V:
                        sE.inc(E.tensor_copy(self.qwdT[:, ib, sl * 256:(sl + 1) * 256],
                                             pb[b][:, 0:256]))
                    else:
                        sE.inc(E.copy(self.qwdT[:, ib, sl * 256:(sl + 1) * 256],
                                      pb[b][:, 0:256]))
                    bank_free[b] = (sE, sE.n)
                qsl_trans[sb_] = sT.n
        if "qwdT" in self.taps:
            SY.wait_ge(sV.h, sV.n)
            SY.wait_ge(sS.h, sS.n)
            sq.inc(SY.dma_start(out=self.taps["qwdT"].rearrange("(a p) h -> p a h", p=P),
                                in_=self.qwdT.ap()), 16)

        SY.wait_ge(sV.h, sV.n)
        SY.wait_ge(sS.h, sS.n)
        SY.wait_ge(sT.h, sT.n)
        NB = 3
        qhT_t = [self.view([P, 32, 128], BF16, 96 + 8 * i) for i in range(NB)]
        s_t = [self.view([P, D], F32, 120 + 4 * i) for i in range(NB)]
        x_t = [self.view([P, D], F32, 132 + 4 * i) for i in range(NB)]
        gm_t = [self.view([P, D], F32, 160), self.view([P, D], F32, 164)]
        with contextlib.ExitStack() as pes:
            pd = [pes.enter_context(nc.psum_tensor(f"pD{i}", [P, 1024], F32))
                  for i in range(2)]
            pd_free = [None, None]
            in_free = [None] * NB
            in_ready = [None] * NB
            gm_free = [None, None]
            out_done = [None] * TC

            def issue_inputs(tc):
                bb = tc % NB
                if in_free[bb] is not None:
                    SY.wait_ge(sV.h, in_free[bb])
                sq.inc(SY.dma_start(out=qhT_t[bb].ap(),
                                    in_=dram["qh_sp"][tc].rearrange("a p t -> p a t")), 16)
                sq.inc(SY.dma_start(out=s_t[bb].ap(),
                                    in_=dram["s_sp"].rearrange("(a p) d -> p a d", p=P)[:, tc, :]), 16)
                sq.inc(SY.dma_start(out=x_t[bb].ap(),
                                    in_=dram["x"].rearrange("(a p) d -> p a d", p=P)[:, tc, :]), 16)
                in_ready[bb] = sq.n

            for tc in range(min(2, TC)):
                issue_inputs(tc)
            for tc in range(TC):
                bb = tc % NB
                po = tc % 2
                if pd_free[po] is not None:
                    T.wait_ge(pd_free[po][0].h, pd_free[po][1])
                T.wait_ge(sq.h, in_ready[bb])
                for ib in range(32):
                    for half in range(2):
                        i = T.matmul(pd[po][:, half * 512:(half + 1) * 512],
                                     qhT_t[bb][:, ib, :],
                                     self.qwdT[:, ib, half * 512:(half + 1) * 512],
                                     start=(ib == 0), stop=(ib == 31))
                sT.inc(i)
                dn_done = sT.n
                if gm_free[bb % 2] is not None:
                    SC.wait_ge(sq.h, gm_free[bb % 2])
                SC.wait_ge(sT.h, dn_done)
                sS.inc(SC.activation(gm_t[bb % 2].ap(), pd[po].ap(), AF.Silu,
                                     scale=st["gsc"][:, tc:tc + 1]))
                pd_free[po] = (sS, sS.n)
                gm_done = sS.n
                V.wait_ge(sS.h, gm_done)
                V.tensor_tensor(gm_t[bb % 2].ap(), gm_t[bb % 2].ap(), s_t[bb].ap(),
                                ALU.mult)
                sV.inc(V.tensor_tensor(gm_t[bb % 2].ap(), gm_t[bb % 2].ap(),
                                       x_t[bb].ap(), ALU.add))
                out_done[tc] = sV.n
                in_free[bb] = out_done[tc]
                if tc + 2 < TC:
                    issue_inputs(tc + 2)
                SY.wait_ge(sV.h, out_done[tc])
                sq.inc(SY.dma_start(
                    out=dram["y"].rearrange("(a p) d -> p a d", p=P)[:, tc, :],
                    in_=gm_t[bb % 2].ap()), 16)
                gm_free[bb % 2] = sq.n
        SY.wait_ge(sq.h, sq.n)


def build(debug=(), stop_after=None):
    return KBuild(debug, stop_after).build()


_NC_CACHE = {}


def _build_nc():
    if "nc" not in _NC_CACHE:
        _NC_CACHE["nc"] = KBuild().build()
    return _NC_CACHE["nc"]


def kernel(L0_attn_x, L0_attn_w_rms_mix, L0_attn_w_rms_ffn, L0_attn_Wg,
           L0_attn_bg, L0_attn_Wv, L0_attn_bv, L0_attn_Wd, L0_attn_bd,
           L0_attn_W_gate, L0_attn_W_up, L0_attn_W_out,
           L0_ffn_w_gate, L0_ffn_w_up, L0_ffn_w_down):
    import numpy as np
    from concourse.bass_utils import run_bass_kernel_spmd

    f32 = lambda a: np.ascontiguousarray(np.asarray(a, dtype=np.float32))
    shared = {
        "wrmix": f32(L0_attn_w_rms_mix), "wrffn": f32(L0_attn_w_rms_ffn),
        "Wg": f32(L0_attn_Wg), "Wv": f32(L0_attn_Wv), "Wd": f32(L0_attn_Wd),
        "bg": f32(L0_attn_bg), "bv": f32(L0_attn_bv), "bd": f32(L0_attn_bd),
        "Wgate": f32(L0_attn_W_gate), "Wup": f32(L0_attn_W_up),
        "Wout": f32(L0_attn_W_out),
        "fwg": f32(L0_ffn_w_gate), "fwu": f32(L0_ffn_w_up),
        "fwd": f32(L0_ffn_w_down),
    }
    x = np.asarray(L0_attn_x, dtype=np.float32)
    in_maps = [dict(shared, x=np.ascontiguousarray(x[b])) for b in range(8)]
    nc = _build_nc()
    res = run_bass_kernel_spmd(nc, in_maps, core_ids=list(range(8)))
    return np.stack([res.results[b]["y"] for b in range(8)], axis=0)


# version 14
# speedup vs baseline: 1.2331x; 1.0098x over previous
import contextlib
import math

import concourse.bass as bass
import concourse.mybir as mybir

F32 = mybir.dt.float32
BF16 = mybir.dt.bfloat16
AF = mybir.ActivationFunctionType
ALU = mybir.AluOpType
AX = mybir.AxisListType

MAGIC = float(3 * 2.0 ** 22)
S, D, FF, II = 2048, 1024, 3072, 4096
TC, P = 16, 128
KB = 1024


class Sem:
    def __init__(self, es, nc, name):
        self.h = es.enter_context(nc.semaphore(name))
        self.n = 0

    def inc(self, instr, k=1):
        instr.then_inc(self.h, k)
        self.n += k
        return instr


class KBuild:
    def __init__(self, debug=(), stop_after=None):
        self.debug = set(debug)
        self.stop_after = stop_after

    def build(self):
        es = contextlib.ExitStack()
        self.es = es
        nc = bass.Bass("TRN2", target_bir_lowering=False,
                       detect_race_conditions=False)
        self.nc = nc
        with es:
            self.declare()
            self.blk = es.enter_context(nc.Block())
            nc.gpsimd.dma_reset()
            for s in self.all_sems:
                nc.gpsimd.sem_clear(s.h)
            nc.vector.memset(self.ceps[:, 0:1], 1e-5)
            nc.vector.memset(self.ceps[:, 1:2], 1e-6)
            nc.vector.memset(self.junkA.ap(), 1.0)
            nc.all_engine_barrier()
            phases = [("A", self.phase_A), ("C1", self.phase_C1),
                      ("C2pre", self.phase_C2pre), ("C2", self.phase_C2),
                      ("D0", self.phase_D0), ("D1", self.phase_D1),
                      ("D2", self.phase_D2)]
            for name, fn in phases:
                fn()
                nc.all_engine_barrier()
                if self.stop_after == name:
                    break
            nc.sync.wait_ge(self.sq.h, self.sq.n)
            nc.gpsimd.wait_ge(self.gq.h, self.gq.n)
        return nc

    def declare(self):
        nc, es = self.nc, self.es
        dt = {}
        self.dram = dt
        ein = lambda n, s: nc.dram_tensor(n, s, F32, kind="ExternalInput").ap()
        dt["x"] = ein("x", [S, D])
        dt["wrmix"] = ein("wrmix", [D])
        dt["wrffn"] = ein("wrffn", [D])
        for n in ("Wg", "Wv", "Wd"):
            dt[n] = ein(n, [D, D])
        for n in ("bg", "bv", "bd"):
            dt[n] = ein(n, [D])
        dt["Wgate"] = ein("Wgate", [D, FF])
        dt["Wup"] = ein("Wup", [D, FF])
        dt["Wout"] = ein("Wout", [FF, D])
        dt["fwg"] = ein("fwg", [II, D])
        dt["fwu"] = ein("fwu", [II, D])
        dt["fwd"] = ein("fwd", [D, II])
        dt["y"] = nc.dram_tensor("y", [S, D], F32, kind="ExternalOutput").ap()
        dt["s_sp"] = nc.dram_tensor("s_sp", [S, D], F32, kind="Internal").ap()
        dt["qx_sp"] = nc.dram_tensor("qx_sp", [D, S], BF16, kind="Internal").ap()
        dt["qh_sp"] = nc.dram_tensor("qh_sp", [TC, 32, P, P], BF16,
                                     kind="Internal").ap()
        self.taps = {}
        tapspec = {
            "xn": ([P, TC * D], F32), "xatt": ([P, TC * D], F32),
            "s": ([S, D], F32), "xnb": ([P, 8, S], BF16), "qx": ([D, S], BF16),
            "stats": ([P, 352], F32), "qwgT": ([D, II], BF16),
            "qwdT": ([II, D], BF16), "h4": ([P, II], F32), "g4": ([P, II], F32), "qh": ([TC, 32, P, P], BF16),
        }
        for name in self.debug:
            shp, dty = tapspec[name]
            self.taps[name] = nc.dram_tensor("tap_" + name, shp, dty,
                                             kind="ExternalOutput").ap()

        self.sq = Sem(es, nc, "sq")
        self.gq = Sem(es, nc, "gq")
        self.sT = Sem(es, nc, "sT")
        self.sV = Sem(es, nc, "sV")
        self.sS = Sem(es, nc, "sS")
        self.sG = Sem(es, nc, "sG")
        self.all_sems = [self.sq, self.gq, self.sT, self.sV, self.sS, self.sG]

        sb = lambda n, shp, dty=F32: es.enter_context(nc.sbuf_tensor(n, shp, dty))
        self.ident_b = sb("ident_b", [P, P], BF16)
        self.ones128 = sb("ones128", [P, P], F32)
        self.identf = sb("identf", [P, P], F32)
        self.wrmix_c = sb("wrmix_c", [P, 8])
        self.wrffn_c = sb("wrffn_c", [P, 8])
        self.bg_c = sb("bg_c", [P, 8])
        self.bv_c = sb("bv_c", [P, 8])
        self.bd_c = sb("bd_c", [P, 8])
        self.st = {}
        for n in ("s1", "s2", "mm", "var", "r_ln", "nb_ln", "s2n", "r_mix",
                  "amax", "sxinv", "sx", "s2f", "r_ffn", "kq", "amaxh",
                  "m2", "gsc", "t0", "t1", "t2", "clipd"):
            self.st[n] = sb("st_" + n, [P, 16])
        self.wq = {}
        for n in ("sabs", "sabsB", "mwg", "mwu", "mwd", "thrg", "nthrg", "thru",
                  "nthru", "thrd", "nthrd"):
            self.wq[n] = sb("wq_" + n, [P, 1])
        self.junkA = sb("junkA", [P, 64])
        self.p1acc = {n: sb("p1acc_" + n, [P, 32]) for n in ("fwg", "fwu", "fwd")}
        self.p1_state = {}
        self.ceps = sb("ceps", [P, 2])
        nc.const_aps.register(F32, 1e-5, self.ceps[:, 0:1])
        nc.const_aps.register(F32, 1e-6, self.ceps[:, 1:2])
        self.ARN = es.enter_context(
            nc.sbuf_tensor("ARN", [P, 192 * KB], mybir.dt.uint8))
        self.arena_base = nc.lookup_mloc(self.ARN).addr
        self._vid = 0

    def sg(self, sem=None, k=1):
        i = self.nc.scalar.activation(self.junkA[:, 0:64], self.junkA[:, 0:64],
                                      AF.Identity)
        if sem is not None:
            sem.inc(i, k)
        return i

    def vg(self, sem=None, k=1):
        i = self.nc.vector.tensor_scalar(self.junkA.ap(), self.junkA.ap(),
                                         1.0, None, ALU.mult)
        if sem is not None:
            sem.inc(i, k)
        return i


    def p1_gen(self, wn, slabs, junk, modes, width):
        nc = self.nc
        V, SC, SY = nc.vector, nc.scalar, nc.sync
        sq, sV, sS = self.sq, self.sV, self.sS
        acc = self.p1acc[wn]
        nsl = (II * D) // (128 * width)
        free = [None, None]
        for s2_ in range(nsl):
            sb_ = s2_ % 2
            mode = modes[s2_ % len(modes)]
            if free[sb_] is not None:
                SY.wait_ge(free[sb_][0].h, free[sb_][1])
            sq.inc(SY.dma_start(
                out=slabs[sb_].ap(),
                in_=self.dram[wn].rearrange("a d -> (a d)").rearrange(
                    "(s p w) -> s p w", p=P, w=width)[s2_]), 16)
            if mode == "V":
                V.wait_ge(sq.h, sq.n)
                V.reduce_sum(acc[:, s2_:s2_ + 1], slabs[sb_].ap(),
                             AX.X, apply_absolute_value=True)
                self.vg()
                self.vg(sV)
                free[sb_] = (sV, sV.n)
            else:
                SC.wait_ge(sq.h, sq.n)
                SC.activation(junk[:, 0:width], slabs[sb_].ap(), AF.Abs,
                              accum_out=acc[:, s2_:s2_ + 1])
                self.sg()
                self.sg(sS)
                free[sb_] = (sS, sS.n)
            yield
        self.p1_state[wn] = (free, nsl)

    def pump(self, gen):
        if gen is not None:
            next(gen, None)

    def drain(self, gen):
        if gen is not None:
            for _ in gen:
                pass

    def view(self, shape, dtype, off_kb):
        self._vid += 1
        return self.nc.alloc_sbuf_tensor_at(
            f"v{self._vid}", shape, dtype,
            offset=self.arena_base + int(round(off_kb * KB)))

    def phase_A(self):
        nc, st, dram = self.nc, self.st, self.dram
        sq, gq, sT, sV, sS, sG = self.sq, self.gq, self.sT, self.sV, self.sS, self.sG
        V, SC, T, SY, G = nc.vector, nc.scalar, nc.tensor, nc.sync, nc.gpsimd

        self.xn_tm = self.view([P, TC, D], F32, 0)
        self.x_tm = self.view([P, TC, D], F32, 64)
        self.xnb = self.view([P, 8, S], BF16, 128)
        xn2b_q = self.view([P, 4, D], BF16, 160)
        qx_q = self.view([P, 4, D], BF16, 168)
        sqscr = self.view([P, D], F32, 176)
        qtmp = self.view([P, D], F32, 180)
        qxstage = self.view([P, 8, 512], BF16, 184)

        sq.inc(SY.dma_start(out=self.x_tm.ap(),
                            in_=dram["x"].rearrange("(a p) d -> p a d", p=P)), 16)
        for nm, dst in (("wrmix", self.wrmix_c), ("wrffn", self.wrffn_c),
                        ("bg", self.bg_c), ("bv", self.bv_c), ("bd", self.bd_c)):
            sq.inc(SY.dma_start(out=dst.ap(),
                                in_=dram[nm].rearrange("(c p) -> p c", p=P),
                                allow_slow_non_contiguous=True), 16)
        x_loaded = sq.n

        G.memset(self.ident_b.ap(), 0.0)
        G.affine_select(out=self.ident_b.ap(), in_=self.ident_b.ap(),
                        compare_op=ALU.not_equal, fill=1.0, base=0,
                        pattern=[[-1, P]], channel_multiplier=1)
        G.memset(self.identf.ap(), 0.0)
        G.affine_select(out=self.identf.ap(), in_=self.identf.ap(),
                        compare_op=ALU.not_equal, fill=1.0, base=0,
                        pattern=[[-1, P]], channel_multiplier=1)
        sG.inc(G.memset(self.ones128.ap(), 1.0))
        self.idents_ready = sG.n

        SC.memzero(st["s2"].ap())
        SC.memzero(st["s2n"].ap())
        SC.memzero(st["s2f"].ap())
        SC.wait_ge(sq.h, x_loaded)
        for tc in range(TC):
            i = SC.activation(sqscr.ap(), self.x_tm[:, tc, :], AF.Square,
                              accum_out=st["s2"][:, tc:tc + 1])
        sS.inc(i)
        s2_done = sS.n

        V.wait_ge(sq.h, x_loaded)
        for tc in range(TC):
            V.reduce_sum(st["s1"][:, tc:tc + 1], self.x_tm[:, tc, :], AX.X)
        V.wait_ge(sS.h, s2_done)
        self.vg()
        self.vg()
        V.tensor_scalar(st["mm"].ap(), st["s1"].ap(), 1.0 / D, None, ALU.mult)
        self.vg()
        V.tensor_tensor(st["t0"].ap(), st["mm"].ap(), st["mm"].ap(), ALU.mult)
        self.vg()
        V.tensor_scalar(st["t0"].ap(), st["t0"].ap(), -1.0, None, ALU.mult)
        self.vg()
        V.scalar_tensor_tensor(st["var"].ap(), st["s2"].ap(), 1.0 / D,
                               st["t0"].ap(), ALU.mult, ALU.add)
        self.vg(sV)
        var_done = sV.n

        SC.wait_ge(sV.h, var_done)
        SC.activation(st["t1"].ap(), st["var"].ap(), AF.Ln, bias=1e-5, scale=1.0)
        self.sg()
        SC.activation(st["r_ln"].ap(), st["t1"].ap(), AF.Exp, scale=-0.5)
        self.sg(sS)
        rln_done = sS.n

        V.wait_ge(sS.h, rln_done)
        V.tensor_tensor(st["nb_ln"].ap(), st["mm"].ap(), st["r_ln"].ap(), ALU.mult)
        self.vg()
        V.tensor_scalar(st["nb_ln"].ap(), st["nb_ln"].ap(), -1.0, None, ALU.mult)
        self.vg(sV)
        nb_done = sV.n

        SC.wait_ge(sV.h, nb_done)
        for tc in range(TC):
            i = SC.activation(self.xn_tm[:, tc, :], self.x_tm[:, tc, :],
                              AF.Identity, bias=st["nb_ln"][:, tc:tc + 1],
                              scale=st["r_ln"][:, tc:tc + 1])
        sS.inc(i)
        xn_done = sS.n
        for tc in range(TC):
            SC.activation(sqscr.ap(), self.xn_tm[:, tc, :], AF.Square,
                          accum_out=st["s2n"][:, tc:tc + 1])
        SC.activation(st["t1"].ap(), st["s2n"].ap(), AF.Ln, bias=1e-6, scale=1.0 / D)
        self.sg()
        SC.activation(st["r_mix"].ap(), st["t1"].ap(), AF.Exp, scale=-0.5)
        self.sg(sS)
        rmix_done = sS.n

        V.wait_ge(sS.h, xn_done)
        for tc in range(TC):
            V.reduce_max(st["amax"][:, tc:tc + 1], self.xn_tm[:, tc, :], AX.X,
                         apply_absolute_value=True)
        self.vg()
        V.tensor_scalar(st["sxinv"].ap(), st["amax"].ap(), 1e-5,
                        1.0 / 127.0, ALU.max, ALU.mult)
        self.vg(sV)
        sxinv_done = sV.n
        SC.wait_ge(sV.h, sxinv_done)
        SC.activation(st["t1"].ap(), st["sxinv"].ap(), AF.Ln)
        self.sg()
        SC.activation(st["sx"].ap(), st["t1"].ap(), AF.Exp, scale=-1.0)
        self.sg(sS)
        sx_done = sS.n

        with contextlib.ExitStack() as pes:
            pb = [pes.enter_context(nc.psum_tensor(f"pA{i}", [P, 512], BF16))
                  for i in range(4)]
            bank_free = [None, None, None, None]
            q_trans_done = None
            for q in range(4):
                V.wait_ge(sS.h, sx_done)
                if q_trans_done is not None:
                    V.wait_ge(sT.h, q_trans_done)
                for k in range(4):
                    tc = q * 4 + k
                    V.tensor_scalar(qtmp.ap(), self.xn_tm[:, tc, :],
                                    st["sx"][:, tc:tc + 1], MAGIC, ALU.mult, ALU.add)
                    i = V.tensor_scalar(qx_q[:, k, :], qtmp.ap(), -MAGIC,
                                        None, ALU.add)
                sV.inc(i)
                qxq_done = sV.n
                SC.wait_ge(sS.h, rmix_done)
                if q_trans_done is not None:
                    SC.wait_ge(sT.h, q_trans_done)
                for k in range(4):
                    tc = q * 4 + k
                    i = SC.activation(xn2b_q[:, k, :], self.xn_tm[:, tc, :],
                                      AF.Identity, scale=st["r_mix"][:, tc:tc + 1])
                sS.inc(i)
                x2b_done = sS.n

                T.wait_ge(sV.h, qxq_done)
                T.wait_ge(sS.h, x2b_done)
                T.wait_ge(sG.h, self.idents_ready)
                for fc in range(8):
                    b = fc % 2
                    if bank_free[b] is not None:
                        T.wait_ge(bank_free[b][0].h, bank_free[b][1])
                    for k in range(4):
                        i = T.transpose(pb[b][:, k * 128:(k + 1) * 128],
                                        xn2b_q[:, k, fc * 128:(fc + 1) * 128],
                                        self.ident_b.ap())
                    sT.inc(i)
                    tcnt = sT.n
                    SC.wait_ge(sT.h, tcnt)
                    sS.inc(SC.activation(self.xnb[:, fc, q * 512:(q + 1) * 512],
                                         pb[b].ap(), AF.Identity,
                                         scale=self.wrmix_c[:, fc:fc + 1]))
                    bank_free[b] = (sS, sS.n)
                for jc in range(8):
                    b = 2 + jc % 2
                    if bank_free[b] is not None:
                        T.wait_ge(bank_free[b][0].h, bank_free[b][1])
                    for k in range(4):
                        i = T.transpose(pb[b][:, k * 128:(k + 1) * 128],
                                        qx_q[:, k, jc * 128:(jc + 1) * 128],
                                        self.ident_b.ap())
                    sT.inc(i)
                    tcnt = sT.n
                    V.wait_ge(sT.h, tcnt)
                    sV.inc(V.tensor_copy(qxstage[:, jc, :], pb[b].ap()))
                    bank_free[b] = (sV, sV.n)
                stage_done = sV.n
                q_trans_done = sT.n
                SY.wait_ge(sV.h, stage_done)
                sq.inc(SY.dma_start(
                    out=dram["qx_sp"].rearrange("(a p) t -> p a t", p=P)[:, :, q * 512:(q + 1) * 512],
                    in_=qxstage.ap()), 16)
        self.qx_spilled = sq.n

        if self.taps:
            SY.wait_ge(sS.h, sS.n)
            SY.wait_ge(sV.h, sV.n)
            if "xn" in self.taps:
                sq.inc(SY.dma_start(out=self.taps["xn"],
                                    in_=self.xn_tm.ap().rearrange("p a d -> p (a d)")), 16)
            if "stats" in self.taps:
                for i2, nm in enumerate(("mm", "var", "r_ln", "r_mix", "amax",
                                         "sxinv", "sx", "s2", "s2n", "s1")):
                    sq.inc(SY.dma_start(out=self.taps["stats"][:, i2 * 16:(i2 + 1) * 16],
                                        in_=st[nm].ap()), 16)
            if "xnb" in self.taps:
                sq.inc(SY.dma_start(out=self.taps["xnb"], in_=self.xnb.ap()), 16)
            if "qx" in self.taps:
                SY.wait_ge(sq.h, self.qx_spilled)
                sq.inc(SY.dma_start(out=self.taps["qx"], in_=dram["qx_sp"]), 16)

    def phase_C1(self):
        nc, st, dram = self.nc, self.st, self.dram
        sq, gq, sT, sV, sS, sG = self.sq, self.gq, self.sT, self.sV, self.sS, self.sG
        V, SC, T, SY, G = nc.vector, nc.scalar, nc.tensor, nc.sync, nc.gpsimd

        self.xatt = self.view([P, TC, D], F32, 64)
        wtile = [self.view([P, 3, 8, 128], BF16, 160),
                 self.view([P, 3, 8, 128], BF16, 166)]
        xs = self.view([P, S], F32, 172)
        ah = self.view([P, S], F32, 180)
        t1 = self.view([P, 512], F32, 188)
        t2 = self.view([P, 512], F32, 190)

        with contextlib.ExitStack() as pes:
            pgvd = [pes.enter_context(nc.psum_tensor(f"pC{i}", [P, 512], F32))
                    for i in range(6)]
            ph = [pes.enter_context(nc.psum_tensor(f"pH{i}", [P, 512], F32))
                  for i in range(2)]
            xnb_done = sS.n
            gvd_free = [None, None]
            ph_free = [None, None]
            wfree = [None, None]
            h_consumed_t = None
            ep_v_last = None

            for jc in range(8):
                wb = jc % 2
                if wfree[wb] is not None:
                    G.wait_ge(sT.h, wfree[wb])
                for wi, wn in enumerate(("Wg", "Wv", "Wd")):
                    gq.inc(G.dma_start(
                        out=wtile[wb][:, wi, :, :],
                        in_=dram[wn][:, jc * 128:(jc + 1) * 128]
                            .rearrange("(a p) j -> p a j", p=P)), 16)
                w_ready = gq.n

                T.wait_ge(gq.h, w_ready)
                for sl in range(4):
                    pset = sl % 2
                    if gvd_free[pset] is not None:
                        T.wait_ge(gvd_free[pset][0].h, gvd_free[pset][1])
                    for wi in range(3):
                        pp = pgvd[pset * 3 + wi]
                        for ic in range(8):
                            i = T.matmul(pp.ap(), wtile[wb][:, wi, ic, :],
                                         self.xnb[:, ic, sl * 512:(sl + 1) * 512],
                                         start=(ic == 0), stop=(ic == 7))
                    sT.inc(i)
                    mm_done = sT.n

                    SC.wait_ge(sT.h, mm_done)
                    if sl == 0 and h_consumed_t is not None:
                        SC.wait_ge(sT.h, h_consumed_t)
                    if ep_v_last is not None:
                        SC.wait_ge(sV.h, ep_v_last)
                    SC.activation(t1.ap(), pgvd[pset * 3 + 0].ap(), AF.Sigmoid,
                                  bias=self.bg_c[:, jc:jc + 1])
                    SC.activation(t2.ap(), pgvd[pset * 3 + 1].ap(), AF.Tanh,
                                  bias=self.bv_c[:, jc:jc + 1])
                    sS.inc(SC.activation(ah[:, sl * 512:(sl + 1) * 512],
                                         pgvd[pset * 3 + 2].ap(), AF.Sigmoid,
                                         bias=self.bd_c[:, jc:jc + 1]))
                    ep_s = sS.n
                    V.wait_ge(sS.h, ep_s)
                    V.tensor_tensor(xs[:, sl * 512:(sl + 1) * 512], t1.ap(),
                                    t2.ap(), ALU.mult)
                    sV.inc(V.tensor_scalar(ah[:, sl * 512:(sl + 1) * 512],
                                           ah[:, sl * 512:(sl + 1) * 512],
                                           0.998, 0.001, ALU.mult, ALU.add))
                    gvd_free[pset] = (sV, sV.n)
                    ep_v_last = sV.n
                sV.inc(V.tensor_tensor_scan(ah.ap(), ah.ap(), xs.ap(), 0.0,
                                            ALU.mult, ALU.add))
                scan_done = sV.n
                T.wait_ge(sV.h, scan_done)
                for tq in range(4):
                    b = tq % 2
                    if ph_free[b] is not None:
                        T.wait_ge(ph_free[b][0].h, ph_free[b][1])
                    for k in range(4):
                        tc = tq * 4 + k
                        i = T.transpose(ph[b][:, k * 128:(k + 1) * 128],
                                        ah[:, tc * 128:(tc + 1) * 128],
                                        self.identf.ap())
                    sT.inc(i)
                    tcnt = sT.n
                    V.wait_ge(sT.h, tcnt)
                    sV.inc(V.tensor_tensor(
                        self.xatt[:, tq * 4:tq * 4 + 4, jc * 128:(jc + 1) * 128],
                        self.xn_tm[:, tq * 4:tq * 4 + 4, jc * 128:(jc + 1) * 128],
                        ph[b].ap().rearrange("p (a t) -> p a t", a=4), ALU.add))
                    ph_free[b] = (sV, sV.n)
                h_consumed_t = sT.n
                wfree[wb] = mm_done

        if "xatt" in self.taps:
            SY.wait_ge(sV.h, sV.n)
            sq.inc(SY.dma_start(out=self.taps["xatt"],
                                in_=self.xatt.ap().rearrange("p a d -> p (a d)")), 16)

    def phase_C2pre(self):
        nc, st, dram = self.nc, self.st, self.dram
        sq, sT, sV, sS = self.sq, self.sT, self.sV, self.sS
        V, SC, T, SY = nc.vector, nc.scalar, nc.tensor, nc.sync
        self.ynb = self.view([P, 8, S], BF16, 128)
        x2b_q = self.view([P, 4, D], BF16, 168)
        sqscr = self.view([P, D], F32, 176)
        self.c2_wgt = [self.view([P, 8, 128], BF16, 160), self.view([P, 8, 128], BF16, 162)]
        self.c2_wut = [self.view([P, 8, 128], BF16, 164), self.view([P, 8, 128], BF16, 166)]
        self.c2_w_ready = [None, None]
        gq = self.gq
        G = nc.gpsimd
        for c in range(2):
            gq.inc(G.dma_start(out=self.c2_wgt[c].ap(),
                               in_=dram["Wgate"][:, c * 128:(c + 1) * 128]
                                   .rearrange("(a p) f -> p a f", p=P)), 16)
            gq.inc(G.dma_start(out=self.c2_wut[c].ap(),
                               in_=dram["Wup"][:, c * 128:(c + 1) * 128]
                                   .rearrange("(a p) f -> p a f", p=P)), 16)
            self.c2_w_ready[c] = gq.n

        for tc in range(TC):
            i = SC.activation(sqscr.ap(), self.xatt[:, tc, :], AF.Square,
                              accum_out=st["s2f"][:, tc:tc + 1])
        SC.activation(st["t1"].ap(), st["s2f"].ap(), AF.Ln, bias=1e-6, scale=1.0 / D)
        self.sg()
        SC.activation(st["r_ffn"].ap(), st["t1"].ap(), AF.Exp, scale=-0.5)
        self.sg(sS)
        rffn_done = sS.n

        with contextlib.ExitStack() as pes:
            pb = [pes.enter_context(nc.psum_tensor(f"pY{i}", [P, 512], BF16))
                  for i in range(2)]
            bank_free = [None, None]
            q_trans_done = None
            for q in range(4):
                SC.wait_ge(sS.h, rffn_done)
                if q_trans_done is not None:
                    SC.wait_ge(sT.h, q_trans_done)
                for k in range(4):
                    tc = q * 4 + k
                    i = SC.activation(x2b_q[:, k, :], self.xatt[:, tc, :],
                                      AF.Identity, scale=st["r_ffn"][:, tc:tc + 1])
                sS.inc(i)
                x2b_done = sS.n
                T.wait_ge(sS.h, x2b_done)
                for fc in range(8):
                    b = fc % 2
                    if bank_free[b] is not None:
                        T.wait_ge(bank_free[b][0].h, bank_free[b][1])
                    for k in range(4):
                        i = T.transpose(pb[b][:, k * 128:(k + 1) * 128],
                                        x2b_q[:, k, fc * 128:(fc + 1) * 128],
                                        self.ident_b.ap())
                    sT.inc(i)
                    SC.wait_ge(sT.h, sT.n)
                    sS.inc(SC.activation(self.ynb[:, fc, q * 512:(q + 1) * 512],
                                         pb[b].ap(), AF.Identity,
                                         scale=self.wrffn_c[:, fc:fc + 1]))
                    bank_free[b] = (sS, sS.n)
                q_trans_done = sT.n

    def phase_C2(self):
        nc, st, dram = self.nc, self.st, self.dram
        sq, gq, sT, sV, sS = self.sq, self.gq, self.sT, self.sV, self.sS
        V, SC, T, SY, G = nc.vector, nc.scalar, nc.tensor, nc.sync, nc.gpsimd
        p1slabs = [self.view([P, 1024], F32, 172), self.view([P, 1024], F32, 176)]
        p1junk = self.view([P, 1024], F32, 180)
        bg_fwg = self.p1_gen("fwg", p1slabs, p1junk, ["S"], 1024)

        H1 = self.view([P, 8, S], BF16, 0)
        woutT = [self.view([P, 8, D], BF16, 32), self.view([P, 8, D], BF16, 48)]
        wgt = self.c2_wgt
        wut = self.c2_wut
        t1 = [self.view([P, 512], F32, 168), self.view([P, 512], F32, 170)]

        with contextlib.ExitStack() as pes:
            pG = [pes.enter_context(nc.psum_tensor(f"pG{i}", [P, 512], F32))
                  for i in range(2)]
            pU = [pes.enter_context(nc.psum_tensor(f"pU{i}", [P, 512], F32))
                  for i in range(2)]
            pO = [pes.enter_context(nc.psum_tensor(f"pO{i}", [P, 1024], F32))
                  for i in range(2)]
            pGU_free = [None, None]
            pO_free = [None, None]
            w_free = [None, None]
            wo_free = [None, None]
            h1_ready_after_third = None

            for cb in range(3):
                wob = cb % 2
                if wo_free[wob] is not None:
                    G.wait_ge(sT.h, wo_free[wob])
                gq.inc(G.dma_start(
                    out=woutT[wob].ap(),
                    in_=dram["Wout"][cb * 1024:(cb + 1) * 1024, :]
                        .rearrange("(a p) j -> p a j", p=P)), 16)
                wout_ready = gq.n
                for c8 in range(8):
                    c = cb * 8 + c8
                    wb = c8 % 2
                    if c < 2:
                        w_ready = self.c2_w_ready[c]
                    else:
                        if w_free[wb] is not None:
                            G.wait_ge(sT.h, w_free[wb])
                        gq.inc(G.dma_start(
                            out=wgt[wb].ap(),
                            in_=dram["Wgate"][:, c * 128:(c + 1) * 128]
                                .rearrange("(a p) f -> p a f", p=P)), 16)
                        gq.inc(G.dma_start(
                            out=wut[wb].ap(),
                            in_=dram["Wup"][:, c * 128:(c + 1) * 128]
                                .rearrange("(a p) f -> p a f", p=P)), 16)
                        w_ready = gq.n
                    T.wait_ge(gq.h, w_ready)
                    for sl in range(4):
                        ps = sl % 2
                        if pGU_free[ps] is not None:
                            T.wait_ge(pGU_free[ps][0].h, pGU_free[ps][1])
                        for ic in range(8):
                            T.matmul(pG[ps].ap(), wgt[wb][:, ic, :],
                                     self.ynb[:, ic, sl * 512:(sl + 1) * 512],
                                     start=(ic == 0), stop=(ic == 7))
                        for ic in range(8):
                            i = T.matmul(pU[ps].ap(), wut[wb][:, ic, :],
                                         self.ynb[:, ic, sl * 512:(sl + 1) * 512],
                                         start=(ic == 0), stop=(ic == 7))
                        sT.inc(i)
                        mm_done = sT.n
                        SC.wait_ge(sT.h, mm_done)
                        sS.inc(SC.activation(t1[ps].ap(), pG[ps].ap(), AF.Silu))
                        sil_done = sS.n
                        V.wait_ge(sS.h, sil_done)
                        sV.inc(V.tensor_tensor(H1[:, c8, sl * 512:(sl + 1) * 512],
                                               t1[ps].ap(), pU[ps].ap(), ALU.mult))
                        pGU_free[ps] = (sV, sV.n)
                    w_free[wb] = sT.n
                    self.pump(bg_fwg)
                h1_done = sV.n
                T.wait_ge(sV.h, h1_done)
                T.wait_ge(gq.h, wout_ready)
                for tc in range(TC):
                    po = tc % 2
                    if pO_free[po] is not None:
                        T.wait_ge(pO_free[po][0].h, pO_free[po][1])
                    for c8 in range(8):
                        for half in range(2):
                            i = T.matmul(pO[po][:, half * 512:(half + 1) * 512],
                                         H1[:, c8, tc * 128:(tc + 1) * 128],
                                         woutT[wob][:, c8, half * 512:(half + 1) * 512],
                                         start=(c8 == 0), stop=(c8 == 7))
                    sT.inc(i)
                    ob_done = sT.n
                    V.wait_ge(sT.h, ob_done)
                    sV.inc(V.tensor_tensor(self.xatt[:, tc, :],
                                           self.xatt[:, tc, :], pO[po].ap(),
                                           ALU.add))
                    pO_free[po] = (sV, sV.n)
                    self.pump(bg_fwg)
                wo_free[wob] = sT.n

        self.drain(bg_fwg)
        SY.wait_ge(sV.h, sV.n)
        sq.inc(SY.dma_start(out=dram["s_sp"].rearrange("(a p) d -> p a d", p=P),
                            in_=self.xatt.ap()), 16)
        self.s_spilled = sq.n
        if "s" in self.taps:
            sq.inc(SY.dma_start(out=self.taps["s"].rearrange("(a p) d -> p a d", p=P),
                                in_=self.xatt.ap()), 16)

    def phase_D0(self):
        nc, st, dram = self.nc, self.st, self.dram
        sq, gq, sT, sV, sS, sG = self.sq, self.gq, self.sT, self.sV, self.sS, self.sG
        V, SC, T, SY, G = nc.vector, nc.scalar, nc.tensor, nc.sync, nc.gpsimd
        wq = self.wq

        psum_scope = contextlib.ExitStack()
        pst = psum_scope.enter_context(nc.psum_tensor("pst", [P, 512], F32))

        def thresholds(wn, mw, thr, nthr):
            f, nsl = self.p1_state[wn]
            T.wait_ge(f[0][0].h, f[0][1])
            T.wait_ge(f[1][0].h, f[1][1])
            T.wait_ge(sG.h, self.idents_ready)
            T.wait_ge(sV.h, sV.n)
            sT.inc(T.matmul(pst[:, 0:nsl], self.ones128.ap(),
                            self.p1acc[wn][:, 0:nsl], start=True, stop=True))
            V.wait_ge(sT.h, sT.n)
            V.reduce_sum(wq[mw].ap(), pst[:, 0:nsl], AX.X)
            self.vg()
            V.tensor_scalar(wq[mw].ap(), wq[mw].ap(), 1.0 / (II * D), 1e-5,
                            ALU.mult, ALU.max)
            self.vg()
            V.tensor_scalar(wq[thr].ap(), wq[mw].ap(), 0.5, None, ALU.mult)
            self.vg()
            V.tensor_scalar(wq[nthr].ap(), wq[thr].ap(), -1.0, None, ALU.mult)
            self.vg(sV)

        thresholds("fwg", "mwg", "thrg", "nthrg")
        fwu_slabs = [self.view([P, 8192], F32, 64), self.view([P, 8192], F32, 96)]
        fwu_junk = self.view([P, 8192], F32, 128)
        bg_fwu = self.p1_gen("fwu", fwu_slabs, fwu_junk, ["S"], 8192)
        fwd_slabs = [self.view([P, 4096], F32, 128), self.view([P, 4096], F32, 144)]
        bg_fwd = self.p1_gen("fwd", fwd_slabs, None, ["V"], 4096)
        self.qwgT = self.view([P, 8, II], BF16, 0)
        self.qwuT = self.view([P, 8, II], BF16, 64)
        wsl = [self.view([P, 2, D], F32, 160), self.view([P, 2, D], F32, 168)]
        ctmp = self.view([P, 2 * D], F32, 176)
        qsl = self.view([P, 2, D], BF16, 184)
        with contextlib.ExitStack() as pes:
            pb = [pes.enter_context(nc.psum_tensor(f"pW{i}", [P, 512], BF16))
                  for i in range(2)]
            slab_free = [None, None]
            bank_free = [None, None]
            last_trans = None
            for mi, (wn, dst, thr, nthr) in enumerate((
                    ("fwg", self.qwgT, "thrg", "nthrg"),
                    ("fwu", self.qwuT, "thru", "nthru"))):
                for sl in range(16):
                    sb_ = sl % 2
                    if slab_free[sb_] is not None:
                        SY.wait_ge(sV.h, slab_free[sb_])
                    sq.inc(SY.dma_start(
                        out=wsl[sb_].ap(),
                        in_=dram[wn].rearrange("(s a p) j -> s p a j", a=2, p=P)[sl]), 16)
                    dma_n = sq.n
                    V.wait_ge(sq.h, dma_n)
                    flat = wsl[sb_].ap().rearrange("p a j -> p (a j)")
                    V.tensor_scalar(ctmp.ap(), flat, wq[thr].ap(), None, ALU.is_gt)
                    V.tensor_scalar(flat, flat, wq[nthr].ap(), None, ALU.is_lt)
                    if last_trans is not None:
                        V.wait_ge(sT.h, last_trans)
                    sV.inc(V.tensor_tensor(qsl.ap().rearrange("p a j -> p (a j)"),
                                           ctmp.ap(), flat, ALU.subtract))
                    q_done = sV.n
                    slab_free[sb_] = q_done
                    slab_free[sb_] = q_done
                    T.wait_ge(sV.h, q_done)
                    for jc in range(8):
                        b = jc % 2
                        if bank_free[b] is not None:
                            T.wait_ge(bank_free[b][0].h, bank_free[b][1])
                        for a in range(2):
                            i = T.transpose(pb[b][:, a * 128:(a + 1) * 128],
                                            qsl[:, a, jc * 128:(jc + 1) * 128],
                                            self.ident_b.ap())
                        sT.inc(i)
                        SC.wait_ge(sT.h, sT.n)
                        sS.inc(SC.copy(dst[:, jc, sl * 256:(sl + 1) * 256],
                                       pb[b][:, 0:256]))
                        bank_free[b] = (sS, sS.n)
                    last_trans = sT.n
                    self.pump(bg_fwu if mi == 0 else bg_fwd)
                if mi == 0:
                    self.drain(bg_fwu)
                    thresholds("fwu", "mwu", "thru", "nthru")
            self.drain(bg_fwd)
            thresholds("fwd", "mwd", "thrd", "nthrd")
        psum_scope.close()

        V.tensor_tensor(st["t0"].ap(), st["sxinv"].ap(), st["sxinv"].ap(), ALU.mult)
        self.vg()
        V.tensor_tensor(st["kq"].ap(), st["t0"].ap(), st["sxinv"].ap(), ALU.mult)
        self.vg()
        V.tensor_scalar(st["kq"].ap(), st["kq"].ap(), wq["mwg"].ap(), None, ALU.mult)
        self.vg()
        V.tensor_scalar(st["kq"].ap(), st["kq"].ap(), wq["mwg"].ap(), None, ALU.mult)
        self.vg()
        V.tensor_scalar(st["kq"].ap(), st["kq"].ap(), wq["mwu"].ap(), None, ALU.mult)
        self.vg(sV)
        if "qwgT" in self.taps:
            SY.wait_ge(sV.h, sV.n)
            SY.wait_ge(sS.h, sS.n)
            sq.inc(SY.dma_start(out=self.taps["qwgT"].rearrange("(a p) i -> p a i", p=P),
                                in_=self.qwgT.ap()), 16)
        if "stats" in self.taps:
            SY.wait_ge(sV.h, sV.n)
            SY.wait_ge(sS.h, sS.n)
            sq.inc(SY.dma_start(out=self.taps["stats"][:, 16:24], in_=acc["fwg"].ap()), 16)
            sq.inc(SY.dma_start(out=self.taps["stats"][:, 24:32], in_=acc["fwd"].ap()), 16)
            for i2, nm in enumerate(("mwg", "thrg", "mwu", "thru", "mwd", "thrd", "sabs")):
                sq.inc(SY.dma_start(out=self.taps["stats"][:, i2:i2 + 1],
                                    in_=wq[nm].ap(),
                                    allow_slow_non_contiguous=True), 16)

    SERIAL_D1 = False

    def phase_D1(self):
        nc, st, dram = self.nc, self.st, self.dram
        sq, sT, sV, sS = self.sq, self.sT, self.sV, self.sS
        V, SC, T, SY = nc.vector, nc.scalar, nc.tensor, nc.sync
        wq = self.wq

        qx_t = [self.view([P, 8, 128], BF16, 160), self.view([P, 8, 128], BF16, 162)]
        qhT = self.view([P, 32, 128], BF16, 176)
        qh_tm = self.view([P, II], BF16, 184)
        g_tm = self.view([P, II], F32, 128)
        h_tm = self.view([P, II], F32, 144)
        qx_used = [None, None]
        qx_ready = [None, None]

        with contextlib.ExitStack() as pes:
            pQ = [pes.enter_context(nc.psum_tensor(f"pQ{i}", [P, 1024], F32))
                  for i in range(3)]
            pqh = pes.enter_context(nc.psum_tensor("pqh", [P, 1024], BF16))
            pQ_free = [None, None, None]
            pqh_free = None
            stage_free = None
            trans_done = None
            pending = None
            prev_qh_done = None
            rot = [0]
            for tc in range(TC):
                qb = tc % 2
                if qx_used[qb] is not None:
                    SY.wait_ge(sT.h, qx_used[qb])
                sq.inc(SY.dma_start(
                    out=qx_t[qb].ap(),
                    in_=dram["qx_sp"].rearrange("(a p) t -> p a t", p=P)[:, :, tc * 128:(tc + 1) * 128]), 16)
                qx_ready[qb] = sq.n
                T.wait_ge(sq.h, qx_ready[qb])
                g_evac_done = [None] * 4
                sq_done_q = [None] * 4
                if prev_qh_done is not None:
                    SC.wait_ge(sV.h, prev_qh_done)
                for qq in range(4):
                    ps = qq % 3
                    if pQ_free[ps] is not None:
                        T.wait_ge(pQ_free[ps][0].h, pQ_free[ps][1])
                    for j in range(8):
                        for nb in range(2):
                            i = T.matmul(pQ[ps][:, nb * 512:(nb + 1) * 512],
                                         qx_t[qb][:, j, :],
                                         self.qwgT[:, j, qq * 1024 + nb * 512:
                                                   qq * 1024 + (nb + 1) * 512],
                                         start=(j == 0), stop=(j == 7))
                    sT.inc(i)
                    SC.wait_ge(sT.h, sT.n)
                    sS.inc(SC.activation(g_tm[:, qq * 1024:(qq + 1) * 1024],
                                         pQ[ps].ap(), AF.Relu))
                    g_evac_done[qq] = sS.n
                    pQ_free[ps] = (sS, sS.n)
                g_done = sV.n
                if "g4" in self.taps and tc == 1:
                    SY.wait_ge(sV.h, g_done)
                    sq.inc(SY.dma_start(out=self.taps["g4"], in_=g_tm.ap()), 16)
                for qq in range(4):
                    sS.inc(SC.activation(h_tm[:, qq * 1024:(qq + 1) * 1024],
                                         g_tm[:, qq * 1024:(qq + 1) * 1024],
                                         AF.Square))
                    sq_done_q[qq] = sS.n
                for qq in range(4):
                    ps = qq % 3
                    if pQ_free[ps] is not None:
                        T.wait_ge(pQ_free[ps][0].h, pQ_free[ps][1])
                    for j in range(8):
                        for nb in range(2):
                            i = T.matmul(pQ[ps][:, nb * 512:(nb + 1) * 512],
                                         qx_t[qb][:, j, :],
                                         self.qwuT[:, j, qq * 1024 + nb * 512:
                                                   qq * 1024 + (nb + 1) * 512],
                                         start=(j == 0), stop=(j == 7))
                    sT.inc(i)
                    V.wait_ge(sT.h, sT.n)
                    V.wait_ge(sS.h, sq_done_q[qq])
                    sV.inc(V.tensor_tensor(h_tm[:, qq * 1024:(qq + 1) * 1024],
                                           h_tm[:, qq * 1024:(qq + 1) * 1024],
                                           pQ[ps].ap(), ALU.mult))
                    pQ_free[ps] = (sV, sV.n)
                qx_used[qb] = sT.n
                if pending is not None:
                    emit_transposes(*pending)
                    pending = None
                V.reduce_max(st["amaxh"][:, tc:tc + 1], h_tm.ap(), AX.X,
                             apply_absolute_value=True)
                self.vg()
                V.tensor_tensor(st["t0"][:, tc:tc + 1], st["kq"][:, tc:tc + 1],
                                st["amaxh"][:, tc:tc + 1], ALU.mult)
                self.vg()
                V.tensor_scalar(st["clipd"][:, tc:tc + 1],
                                st["t0"][:, tc:tc + 1], 1e-5, None, ALU.max)
                self.vg(sV)
                clip_done = sV.n
                SC.wait_ge(sV.h, clip_done)
                SC.activation(st["t1"][:, tc:tc + 1], st["clipd"][:, tc:tc + 1], AF.Ln)
                self.sg()
                SC.activation(st["t2"][:, tc:tc + 1], st["t1"][:, tc:tc + 1],
                              AF.Exp, scale=-1.0)
                self.sg(sS)
                sh_done = sS.n
                V.wait_ge(sS.h, sh_done)
                V.tensor_scalar(st["t2"][:, tc:tc + 1], st["t2"][:, tc:tc + 1],
                                127.0, None, ALU.mult)
                self.vg()
                V.tensor_tensor(st["m2"][:, tc:tc + 1], st["kq"][:, tc:tc + 1],
                                st["t2"][:, tc:tc + 1], ALU.mult)
                self.vg()
                V.tensor_scalar(st["gsc"][:, tc:tc + 1], st["clipd"][:, tc:tc + 1],
                                1.0 / 127.0, None, ALU.mult)
                self.vg()
                V.tensor_scalar(st["gsc"][:, tc:tc + 1],
                                st["gsc"][:, tc:tc + 1], wq["mwd"].ap(),
                                None, ALU.mult)
                self.vg(sV)
                if trans_done is not None:
                    V.wait_ge(sT.h, trans_done)
                V.tensor_scalar(g_tm.ap(), h_tm.ap(), st["m2"][:, tc:tc + 1],
                                MAGIC, ALU.mult, ALU.add)
                sV.inc(V.tensor_scalar(qh_tm.ap(), g_tm.ap(), -MAGIC, None, ALU.add))
                qh_done = sV.n
                prev_qh_done = qh_done
                pend = (tc, qh_done)
                if "h4" in self.taps and tc == 1:
                    SY.wait_ge(sV.h, qh_done)
                    sq.inc(SY.dma_start(out=self.taps["h4"], in_=h_tm.ap()), 16)

                def emit_transposes(ptc, pqh_done):
                    nonlocal pqh_free, stage_free, trans_done
                    T.wait_ge(sV.h, pqh_done)
                    if stage_free is not None:
                        T.wait_ge(sq.h, stage_free)
                    for grp in range(4):
                        if pqh_free is not None:
                            T.wait_ge(pqh_free[0].h, pqh_free[1])
                        for k in range(8):
                            ib = grp * 8 + k
                            i = T.transpose(pqh[:, k * 128:(k + 1) * 128],
                                            qh_tm[:, ib * 128:(ib + 1) * 128],
                                            self.ident_b.ap())
                        sT.inc(i)
                        V.wait_ge(sT.h, sT.n)
                        sV.inc(V.tensor_copy(
                            qhT[:, grp * 8:(grp + 1) * 8, :],
                            pqh.ap().rearrange("p (k t) -> p k t", k=8)))
                        pqh_free = (sV, sV.n)
                    trans_done = sT.n
                    SY.wait_ge(sV.h, sV.n)
                    sq.inc(SY.dma_start(
                        out=dram["qh_sp"][ptc].rearrange("a p t -> p a t"),
                        in_=qhT.ap()), 16)
                    stage_free = sq.n
                if tc == TC - 1:
                    emit_transposes(*pend)
                    pending = None
                else:
                    pending = pend
        if "qh" in self.taps:
            SY.wait_ge(sq.h, sq.n)
            sq.inc(SY.dma_start(out=self.taps["qh"], in_=dram["qh_sp"]), 16)
        if "stats" in self.taps:
            SY.wait_ge(sV.h, sV.n)
            SY.wait_ge(sS.h, sS.n)
            for i2, nm in enumerate(("amaxh", "kq", "m2", "gsc", "clipd", "t2")):
                sq.inc(SY.dma_start(
                    out=self.taps["stats"][:, 160 + i2 * 16:160 + (i2 + 1) * 16],
                    in_=st[nm].ap()), 16)

    def phase_D2(self):
        nc, st, dram = self.nc, self.st, self.dram
        sq, sT, sV, sS = self.sq, self.sT, self.sV, self.sS
        V, SC, T, SY = nc.vector, nc.scalar, nc.tensor, nc.sync
        wq = self.wq

        self.qwdT = self.view([P, 32, D], BF16, 0)
        wsl = [self.view([P, 2, II], F32, 64), self.view([P, 2, II], F32, 96)]
        ctmp = self.view([P, 2 * II], F32, 160)
        qsl2 = [self.view([P, 2, II], BF16, 128), self.view([P, 2, II], BF16, 144)]
        with contextlib.ExitStack() as pes:
            pb = [pes.enter_context(nc.psum_tensor(f"pWD{i}", [P, 512], BF16))
                  for i in range(2)]
            slab_free = [None, None]
            bank_free = [None, None]
            qsl_trans = [None, None]
            for sl in range(4):
                sb_ = sl % 2
                if slab_free[sb_] is not None:
                    SY.wait_ge(sV.h, slab_free[sb_])
                sq.inc(SY.dma_start(
                    out=wsl[sb_].ap(),
                    in_=dram["fwd"].rearrange("(s a p) i -> s p a i", a=2, p=P)[sl]), 16)
                V.wait_ge(sq.h, sq.n)
                flat = wsl[sb_].ap().rearrange("p a i -> p (a i)")
                V.tensor_scalar(ctmp.ap(), flat, wq["thrd"].ap(), None, ALU.is_gt)
                V.tensor_scalar(flat, flat, wq["nthrd"].ap(), None, ALU.is_lt)
                if qsl_trans[sb_] is not None:
                    V.wait_ge(sT.h, qsl_trans[sb_])
                sV.inc(V.tensor_tensor(qsl2[sb_].ap().rearrange("p a i -> p (a i)"),
                                       ctmp.ap(), flat, ALU.subtract))
                q_done = sV.n
                slab_free[sb_] = q_done
                T.wait_ge(sV.h, q_done)
                for ib in range(32):
                    b = ib % 2
                    if bank_free[b] is not None:
                        T.wait_ge(bank_free[b][0].h, bank_free[b][1])
                    for a in range(2):
                        i = T.transpose(pb[b][:, a * 128:(a + 1) * 128],
                                        qsl2[sb_][:, a, ib * 128:(ib + 1) * 128],
                                        self.ident_b.ap())
                    sT.inc(i)
                    eng = (V, sV) if ib % 2 == 0 else (SC, sS)
                    E, sE = eng
                    E.wait_ge(sT.h, sT.n)
                    if E is V:
                        sE.inc(E.tensor_copy(self.qwdT[:, ib, sl * 256:(sl + 1) * 256],
                                             pb[b][:, 0:256]))
                    else:
                        sE.inc(E.copy(self.qwdT[:, ib, sl * 256:(sl + 1) * 256],
                                      pb[b][:, 0:256]))
                    bank_free[b] = (sE, sE.n)
                qsl_trans[sb_] = sT.n
        if "qwdT" in self.taps:
            SY.wait_ge(sV.h, sV.n)
            SY.wait_ge(sS.h, sS.n)
            sq.inc(SY.dma_start(out=self.taps["qwdT"].rearrange("(a p) h -> p a h", p=P),
                                in_=self.qwdT.ap()), 16)

        SY.wait_ge(sV.h, sV.n)
        SY.wait_ge(sS.h, sS.n)
        SY.wait_ge(sT.h, sT.n)
        NB = 3
        qhT_t = [self.view([P, 32, 128], BF16, 96 + 8 * i) for i in range(NB)]
        s_t = [self.view([P, D], F32, 120 + 4 * i) for i in range(NB)]
        x_t = [self.view([P, D], F32, 132 + 4 * i) for i in range(NB)]
        gm_t = [self.view([P, D], F32, 160), self.view([P, D], F32, 164)]
        with contextlib.ExitStack() as pes:
            pd = [pes.enter_context(nc.psum_tensor(f"pD{i}", [P, 1024], F32))
                  for i in range(2)]
            pd_free = [None, None]
            in_free = [None] * NB
            in_ready = [None] * NB
            gm_free = [None, None]
            out_done = [None] * TC

            def issue_inputs(tc):
                bb = tc % NB
                if in_free[bb] is not None:
                    SY.wait_ge(sV.h, in_free[bb])
                sq.inc(SY.dma_start(out=qhT_t[bb].ap(),
                                    in_=dram["qh_sp"][tc].rearrange("a p t -> p a t")), 16)
                sq.inc(SY.dma_start(out=s_t[bb].ap(),
                                    in_=dram["s_sp"].rearrange("(a p) d -> p a d", p=P)[:, tc, :]), 16)
                sq.inc(SY.dma_start(out=x_t[bb].ap(),
                                    in_=dram["x"].rearrange("(a p) d -> p a d", p=P)[:, tc, :]), 16)
                in_ready[bb] = sq.n

            for tc in range(min(2, TC)):
                issue_inputs(tc)
            for tc in range(TC):
                bb = tc % NB
                po = tc % 2
                if pd_free[po] is not None:
                    T.wait_ge(pd_free[po][0].h, pd_free[po][1])
                T.wait_ge(sq.h, in_ready[bb])
                for ib in range(32):
                    for half in range(2):
                        i = T.matmul(pd[po][:, half * 512:(half + 1) * 512],
                                     qhT_t[bb][:, ib, :],
                                     self.qwdT[:, ib, half * 512:(half + 1) * 512],
                                     start=(ib == 0), stop=(ib == 31))
                sT.inc(i)
                dn_done = sT.n
                if gm_free[bb % 2] is not None:
                    SC.wait_ge(sq.h, gm_free[bb % 2])
                SC.wait_ge(sT.h, dn_done)
                sS.inc(SC.activation(gm_t[bb % 2].ap(), pd[po].ap(), AF.Silu,
                                     scale=st["gsc"][:, tc:tc + 1]))
                pd_free[po] = (sS, sS.n)
                gm_done = sS.n
                V.wait_ge(sS.h, gm_done)
                V.tensor_tensor(gm_t[bb % 2].ap(), gm_t[bb % 2].ap(), s_t[bb].ap(),
                                ALU.mult)
                sV.inc(V.tensor_tensor(gm_t[bb % 2].ap(), gm_t[bb % 2].ap(),
                                       x_t[bb].ap(), ALU.add))
                out_done[tc] = sV.n
                in_free[bb] = out_done[tc]
                if tc + 2 < TC:
                    issue_inputs(tc + 2)
                SY.wait_ge(sV.h, out_done[tc])
                sq.inc(SY.dma_start(
                    out=dram["y"].rearrange("(a p) d -> p a d", p=P)[:, tc, :],
                    in_=gm_t[bb % 2].ap()), 16)
                gm_free[bb % 2] = sq.n
        SY.wait_ge(sq.h, sq.n)


def build(debug=(), stop_after=None):
    return KBuild(debug, stop_after).build()


_NC_CACHE = {}


def _build_nc():
    if "nc" not in _NC_CACHE:
        _NC_CACHE["nc"] = KBuild().build()
    return _NC_CACHE["nc"]


def kernel(L0_attn_x, L0_attn_w_rms_mix, L0_attn_w_rms_ffn, L0_attn_Wg,
           L0_attn_bg, L0_attn_Wv, L0_attn_bv, L0_attn_Wd, L0_attn_bd,
           L0_attn_W_gate, L0_attn_W_up, L0_attn_W_out,
           L0_ffn_w_gate, L0_ffn_w_up, L0_ffn_w_down):
    import numpy as np
    from concourse.bass_utils import run_bass_kernel_spmd

    f32 = lambda a: np.ascontiguousarray(np.asarray(a, dtype=np.float32))
    shared = {
        "wrmix": f32(L0_attn_w_rms_mix), "wrffn": f32(L0_attn_w_rms_ffn),
        "Wg": f32(L0_attn_Wg), "Wv": f32(L0_attn_Wv), "Wd": f32(L0_attn_Wd),
        "bg": f32(L0_attn_bg), "bv": f32(L0_attn_bv), "bd": f32(L0_attn_bd),
        "Wgate": f32(L0_attn_W_gate), "Wup": f32(L0_attn_W_up),
        "Wout": f32(L0_attn_W_out),
        "fwg": f32(L0_ffn_w_gate), "fwu": f32(L0_ffn_w_up),
        "fwd": f32(L0_ffn_w_down),
    }
    x = np.asarray(L0_attn_x, dtype=np.float32)
    in_maps = [dict(shared, x=np.ascontiguousarray(x[b])) for b in range(8)]
    nc = _build_nc()
    res = run_bass_kernel_spmd(nc, in_maps, core_ids=list(range(8)))
    return np.stack([res.results[b]["y"] for b in range(8)], axis=0)
